# version 04
# baseline (speedup 1.0000x reference)
import numpy as np
import ml_dtypes
from contextlib import ExitStack

import concourse.bass as bass
import concourse.mybir as mybir
from concourse.bass_utils import run_bass_kernel_spmd

F32 = mybir.dt.float32
BF16 = mybir.dt.bfloat16
U32 = mybir.dt.uint32
AF = mybir.ActivationFunctionType
ALU = mybir.AluOpType
AX = mybir.AxisListType

D = 2048
NT = 1024
NTC = 8
NDC = 16
NE = 32
RSQ = float(np.float32(1.0) / np.float32(np.sqrt(np.float32(128.0))))
EPS = 1e-5
NEG = -1.0e9

DBG = False


def build_nc():
    nc = bass.Bass("TRN2", target_bir_lowering=False, debug=False)

    x_in = nc.dram_tensor("x_in", [NT, D], F32, kind="ExternalInput").ap()
    Wq = nc.dram_tensor("Wq", [D, D], F32, kind="ExternalInput").ap()
    Wk = nc.dram_tensor("Wk", [D, 128], F32, kind="ExternalInput").ap()
    Wv = nc.dram_tensor("Wv", [D, 128], F32, kind="ExternalInput").ap()
    Wo = nc.dram_tensor("Wo", [D, D], F32, kind="ExternalInput").ap()
    bo = nc.dram_tensor("bo", [1, D], F32, kind="ExternalInput").ap()
    gw = nc.dram_tensor("gw", [NE, D], F32, kind="ExternalInput").ap()
    gup = nc.dram_tensor("gup", [NE, 1536, D], F32, kind="ExternalInput").ap()
    dwn = nc.dram_tensor("dwn", [NE, D, 768], F32, kind="ExternalInput").ap()
    sim = nc.dram_tensor("sim", [NE, NE], F32, kind="ExternalInput").ap()
    mask_ev = nc.dram_tensor("mask_ev", [128, 128], F32, kind="ExternalInput").ap()
    mask_od = nc.dram_tensor("mask_od", [128, 128], F32, kind="ExternalInput").ap()
    out = nc.dram_tensor("out", [NT, D], F32, kind="ExternalOutput").ap()
    dbg_outs = []
    if DBG:
        for i in range(2):
            dbg_outs.append(nc.dram_tensor(f"dbg{i}", [NT, D], F32,
                                           kind="ExternalOutput").ap())

    gup_bf = nc.dram_tensor("gup_bf", [NE, 1536, D], BF16, kind="Internal").ap()
    dwn_bf = nc.dram_tensor("dwn_bf", [NE, D, 768], BF16, kind="Internal").ap()
    wqbf = nc.dram_tensor("wqbf", [128, NDC, D], BF16, kind="Internal").ap()
    wobf = nc.dram_tensor("wobf", [2, 128, 8, D], BF16, kind="Internal").ap()
    wkbf = nc.dram_tensor("wkbf", [128, NDC, 128], BF16, kind="Internal").ap()
    wvbf = nc.dram_tensor("wvbf", [128, NDC, 128], BF16, kind="Internal").ap()
    kv_loc = nc.dram_tensor("kv_loc", [NT, 256], BF16, kind="Internal").ap()
    kv_all = nc.dram_tensor("kv_all", [2 * NT, 256], BF16, kind="Internal").ap()

    I128b = nc.inline_tensor(np.eye(128, dtype=ml_dtypes.bfloat16), name="I128b").ap()
    I128f = nc.inline_tensor(np.eye(128, dtype=np.float32), name="I128f").ap()
    I32f = nc.inline_tensor(np.eye(32, dtype=np.float32), name="I32f").ap()
    I32b = nc.inline_tensor(np.eye(32, dtype=ml_dtypes.bfloat16), name="I32b").ap()
    ones1b = nc.inline_tensor(np.ones((1, 128), dtype=ml_dtypes.bfloat16), name="ones1b").ap()

    x3 = x_in.rearrange("(c p) d -> p c d", p=128)
    out3 = out.rearrange("(c p) d -> p c d", p=128)
    gup4 = gup.rearrange("e (a r) d -> e a r d", r=384)
    kvl3 = kv_loc.rearrange("(c p) n -> p c n", p=128)
    kva_v = kv_all.rearrange("(c p) n -> p c n", p=128)
    wq_r = Wq.rearrange("(c p) d -> p c d", p=128)
    wk_r = Wk.rearrange("(c p) d -> p c d", p=128)
    wv_r = Wv.rearrange("(c p) d -> p c d", p=128)
    wo_r = Wo.rearrange("(g hc p) d -> g p hc d", g=2, p=128)

    ctx = ExitStack()
    with ctx:
        e_ = ctx.enter_context

        def sbt(name, shape, dt):
            return e_(nc.sbuf_tensor(name, shape, dt))

        x_sb = sbt("x_sb", [128, NTC, D], F32)
        zT = sbt("zT", [128, NDC, NT], BF16)
        zc = sbt("zc", [128, D], BF16)
        AR1 = sbt("AR1", [128, 24576], BF16)
        AR2 = sbt("AR2", [128, 20480], BF16)
        silu_t = sbt("silu_t", [128, 2, 512], F32)
        cw = sbt("cw", [128, NTC, NE], F32)
        gwT = sbt("gwT", [128, NDC, NE], BF16)
        borep = sbt("borep", [128, D], BF16)
        st = sbt("st", [128, 64], F32)
        exps = sbt("exps", [128, NE], F32)
        topv = sbt("topv", [128, 8], F32)
        topiu = sbt("topiu", [128, 8], U32)
        topif = sbt("topif", [128, 8], F32)
        w8 = sbt("w8", [128, 8], F32)
        ohp = sbt("ohp", [128, 4, NE], F32)
        ohb = sbt("ohb", [128, NE], F32)
        scand = sbt("scand", [128, 16], F32)
        tmp32 = sbt("tmp32", [128, NE], F32)
        tmpk = sbt("tmpk", [128, 8], F32)
        iota32 = sbt("iota32", [128, NE], F32)
        iota4m = sbt("iota4m", [128, 4], F32)
        ident_b = sbt("ident_b", [128, 128], BF16)
        ident_f = sbt("ident_f", [128, 128], F32)
        ident32 = sbt("ident32", [32, 32], F32)
        ident32b = sbt("ident32b", [32, 32], BF16)
        ones_b = sbt("ones_b", [1, 128], BF16)
        mke = sbt("mke", [128, 128], F32)
        mko = sbt("mko", [128, 128], F32)
        simd_sb = sbt("simd_sb", [32, 32], F32)
        sim_t = sbt("sim_t", [32, 32], F32)
        d32 = sbt("d32", [32, 32], F32)
        m32 = sbt("m32", [32, 32], F32)
        ohsT = sbt("ohsT", [32, 4, 128], F32)
        Rsb = sbt("Rsb", [32, 4, 128], F32)
        Gk = sbt("Gk", [128, 4, NE], F32)

        gwbf = AR1[0:32, 0:2048]
        bo_bf = AR1[0:1, 4096:6144]

        wgT = AR1[:].rearrange("p (b c f) -> p b c f", b=2, c=NDC)
        wq_v = AR1[:, 0:16384].rearrange("p (c n) -> p c n", c=NDC)
        wo_v = AR1[:, 0:16384].rearrange("p (c n) -> p c n", c=8)
        wk_v = AR1[:, 16384:18432].rearrange("p (c n) -> p c n", c=NDC)
        wvv = AR1[:, 18432:20480].rearrange("p (c n) -> p c n", c=NDC)
        kT_v = AR1[:, 16384:18432]
        v_sb = AR1[:, 18432:20480].rearrange("p (c n) -> p c n", c=NDC)
        kv_sb = AR1[:, 20480:22528].rearrange("p (c n) -> p c n", c=NTC)

        wdT = AR2[:, 0:12288].rearrange("p (c n) -> p c n", c=6)
        hT = AR2[:, 12288:18432].rearrange("p (c n) -> p c n", c=6)
        qT = AR2[:, 0:8192].rearrange("p (c n) -> p c n", c=8)
        AT = AR2[:, 8192:16384].rearrange("p (c n) -> p c n", c=16)
        A1 = AR2[:, 16384:18432]
        A2 = AR2[:, 18432:20480]

        PS1 = e_(nc.psum_tensor("PS1", [128, 2048], F32))
        PS2 = e_(nc.psum_tensor("PS2", [128, 2048], F32))

        def psbf(slot):
            return PS2[:, slot * 512:slot * 512 + 64].bitcast(BF16)

        psbf32 = PS2[:, 0:16].bitcast(BF16)

        class Ctr:
            def __init__(self, name):
                self.h = e_(nc.semaphore(name))
                self.v = 0

        TE = Ctr("TE"); V = Ctr("V"); SC = Ctr("SC"); G = Ctr("G")
        LG = Ctr("LG"); LY = Ctr("LY"); CC = Ctr("CC")

        def mk(ctr, instr, k=1):
            instr.then_inc(ctr.h, k)
            ctr.v += k
            return ctr.v

        block = e_(nc.Block())
        on_t, on_v, on_s, on_g, on_y = (block.tensor, block.vector, block.scalar,
                                        block.gpsimd, block.sync)
        mile = {}

        def barrier():
            tev, vv, scv, gv = TE.v, V.v, SC.v, G.v

            def w(x):
                x.wait_ge(TE.h, tev)
                x.wait_ge(V.h, vv)
                x.wait_ge(SC.h, scv)
                x.wait_ge(G.h, gv)
            on_t(w); on_v(w); on_s(w); on_g(w); on_y(w)

        @on_y
        def _(y):
            y.dma_start(out=x_sb[:], in_=x3).then_inc(LY.h, 16); LY.v += 16
            for src, dst in [(I128b, ident_b[:]), (I128f, ident_f[:]),
                             (I32f, ident32[:]), (I32b, ident32b[:]),
                             (ones1b, ones_b[:]),
                             (mask_ev, mke[:]), (mask_od, mko[:]),
                             (sim, sim_t[:])]:
                y.dma_start(out=dst, in_=src).then_inc(LY.h, 16); LY.v += 16
            mile["p0_loads"] = LY.v

        @on_g
        def _(g):
            mk(G, g.iota(iota32[:], pattern=[[1, NE]], base=0, channel_multiplier=0,
                         allow_small_or_imprecise_dtypes=True))
            mk(G, g.iota(iota4m[:], pattern=[[1, 4]], base=-100, channel_multiplier=0,
                         allow_small_or_imprecise_dtypes=True))
            mk(G, g.iota(d32[:], pattern=[[1, 32]], base=0, channel_multiplier=-1,
                         allow_small_or_imprecise_dtypes=True))
            g.dma_start(out=gwbf, in_=gw).then_inc(LG.h, 16); LG.v += 16
            g.dma_start(out=bo_bf, in_=bo).then_inc(LG.h, 16); LG.v += 16
            mile["gw_cast"] = LG.v
            g.dma_start(out=wqbf, in_=wq_r).then_inc(LG.h, 16); LG.v += 16
            g.dma_start(out=wobf[0], in_=wo_r[0]).then_inc(LG.h, 16); LG.v += 16
            g.dma_start(out=wobf[1], in_=wo_r[1]).then_inc(LG.h, 16); LG.v += 16
            g.dma_start(out=wkbf, in_=wk_r).then_inc(LG.h, 16); LG.v += 16
            g.dma_start(out=wvbf, in_=wv_r).then_inc(LG.h, 16); LG.v += 16
            mile["attw"] = LG.v
            for e in range(NE):
                g.dma_start(out=gup_bf[e], in_=gup[e]).then_inc(LG.h, 16)
                g.dma_start(out=dwn_bf[e], in_=dwn[e]).then_inc(LG.h, 16)
                LG.v += 32
                mile[("cast", e)] = LG.v


        @on_v
        def _(v):
            v.wait_ge(G.h, 3)
            v.wait_ge(LY.h, mile["p0_loads"])
            mk(V, v.tensor_single_scalar(m32[:], d32[:], 0.0, op=ALU.is_equal))
            v.wait_ge(V.h, V.v)
            mk(V, v.tensor_scalar(d32[:], m32[:], -1.0, 1.0, op0=ALU.mult, op1=ALU.add))
            v.wait_ge(V.h, V.v)
            mk(V, v.tensor_mul(simd_sb[:], sim_t[:], d32[:]))
            v.wait_ge(V.h, V.v)
            mk(V, v.tensor_add(simd_sb[:], simd_sb[:], m32[:]))
            mile["simdiag"] = V.v

        for dc in range(NDC):
            @on_t
            def _(t, dc=dc):
                if dc == 0:
                    t.wait_ge(LG.h, mile["gw_cast"])
                    t.wait_ge(LY.h, mile["p0_loads"])
                else:
                    t.wait_ge(SC.h, mile[("gwT_e", dc - 1)])
                mile[("gwT_t", dc)] = mk(TE, t.transpose(
                    psbf32, gwbf[:, dc * 128:(dc + 1) * 128], ident32b[:]))

            @on_s
            def _(s, dc=dc):
                s.wait_ge(TE.h, mile[("gwT_t", dc)])
                mile[("gwT_e", dc)] = mk(SC, s.copy(gwT[:, dc, :], psbf32))
        mile["gwT"] = SC.v

        @on_t
        def _(t):
            t.wait_ge(LY.h, mile["p0_loads"])
            t.wait_ge(LG.h, mile["gw_cast"])
            mm = None
            for j in range(4):
                mm = t.matmul(PS1[:, j * 512:(j + 1) * 512], ones_b[:],
                              bo_bf[:, j * 512:(j + 1) * 512], start=True, stop=True)
            mile["borep_mm"] = mk(TE, mm)

        @on_s
        def _(s):
            s.wait_ge(TE.h, mile["borep_mm"])
            mile["borep"] = mk(SC, s.copy(borep[:], PS1[:, 0:2048]))

        barrier()

        def emit_ln_chunk(c, sub):
            @on_v
            def _(v):
                mk(V, v.reduce_sum(st[:, 0:1], x_sb[:, c, :], axis=AX.X))
                mile[("ln_sum", sub, c)] = V.v

            @on_s
            def _(s):
                s.wait_ge(V.h, mile[("ln_sum", sub, c)])
                mk(SC, s.activation(PS1[:, 0:2048], x_sb[:, c, :], AF.Square,
                                    accum_out=st[:, 1:2]))
                mile[("ln_ssq", sub, c)] = SC.v

            @on_v
            def _(v):
                v.wait_ge(SC.h, mile[("ln_ssq", sub, c)])
                mk(V, v.tensor_scalar_mul(st[:, 2:3], st[:, 0:1], -1.0 / D))
                v.wait_ge(V.h, V.v)
                mk(V, v.tensor_mul(st[:, 3:4], st[:, 2:3], st[:, 2:3]))
                v.wait_ge(V.h, V.v)
                mk(V, v.tensor_scalar_mul(st[:, 1:2], st[:, 1:2], 1.0 / D))
                v.wait_ge(V.h, V.v)
                mk(V, v.tensor_sub(st[:, 3:4], st[:, 1:2], st[:, 3:4]))
                v.wait_ge(V.h, V.v)
                mk(V, v.tensor_scalar_add(st[:, 3:4], st[:, 3:4], EPS))
                mile[("ln_var", sub, c)] = V.v

            @on_s
            def _(s):
                s.wait_ge(V.h, mile[("ln_var", sub, c)])
                mk(SC, s.activation(st[:, 4:5], st[:, 3:4], AF.Sqrt))
                mile[("ln_sd", sub, c)] = SC.v

            @on_v
            def _(v):
                v.wait_ge(SC.h, mile[("ln_sd", sub, c)])
                mk(V, v.reciprocal(st[:, 5:6], st[:, 4:5]))
                v.wait_ge(V.h, V.v)
                mk(V, v.tensor_mul(st[:, 6:7], st[:, 2:3], st[:, 5:6]))
                v.wait_ge(V.h, V.v)
                if c > 0:
                    v.wait_ge(TE.h, mile[("zT_t", sub, c - 1, NDC - 1)])
                mk(V, v.tensor_scalar(zc[:], x_sb[:, c, :], st[:, 5:6], st[:, 6:7],
                                      op0=ALU.mult, op1=ALU.add))
                mile[("ln_z", sub, c)] = V.v

            for dc in range(NDC):
                @on_t
                def _(t, dc=dc):
                    if dc == 0:
                        t.wait_ge(V.h, mile[("ln_z", sub, c)])
                    slot = dc % 2
                    if ("zT_e", sub, c, dc - 2) in mile:
                        t.wait_ge(SC.h, mile[("zT_e", sub, c, dc - 2)])
                    elif c > 0 and dc < 2:
                        t.wait_ge(SC.h, mile[("zT_e", sub, c - 1, NDC - 2 + dc)])
                    mile[("zT_t", sub, c, dc)] = mk(TE, t.transpose(
                        psbf(slot), zc[:, dc * 128:(dc + 1) * 128], ident_b[:]))

                @on_s
                def _(s, dc=dc):
                    slot = dc % 2
                    s.wait_ge(TE.h, mile[("zT_t", sub, c, dc)])
                    mile[("zT_e", sub, c, dc)] = mk(SC, s.copy(
                        zT[:, dc, c * 128:(c + 1) * 128], psbf(slot)))
            mile[("zT_done", sub, c)] = SC.v

        def emit_router_chunk(c, sub):
            lgts = PS2[:, 1024:1056]
            sm_a = PS2[0:32, 1536:1664]
            sm_b = PS2[0:32, 1664:1792]
            sm_c = PS2[:, 1792:1824]

            @on_t
            def _(t):
                t.wait_ge(SC.h, mile[("zT_done", sub, c)])
                if ("rt_lg_read", sub, c - 1) in mile:
                    t.wait_ge(SC.h, mile[("rt_lg_read", sub, c - 1)])
                mm = None
                for dc in range(NDC):
                    mm = t.matmul(lgts, zT[:, dc, c * 128:(c + 1) * 128],
                                  gwT[:, dc, :], start=(dc == 0), stop=(dc == NDC - 1))
                mile[("rt_lg", sub, c)] = mk(TE, mm)

            @on_v
            def _(v):
                v.wait_ge(TE.h, mile[("rt_lg", sub, c)])
                mk(V, v.tensor_reduce(st[:, 8:9], lgts, axis=AX.X, op=ALU.max,
                                      negate=True))
                mile[("rt_nm", sub, c)] = V.v

            @on_s
            def _(s):
                s.wait_ge(V.h, mile[("rt_nm", sub, c)])
                mk(SC, s.activation(exps[:], lgts, AF.Exp, bias=st[:, 8:9]))
                mile[("rt_lg_read", sub, c)] = SC.v
                mile[("rt_exp", sub, c)] = SC.v

            @on_v
            def _(v):
                v.wait_ge(SC.h, mile[("rt_exp", sub, c)])
                mk(V, v.max(out=topv[:], in_=exps[:]))
                v.wait_ge(V.h, V.v)
                mk(V, v.max_index(out=topiu[:], in_max=topv[:], in_values=exps[:]))
                v.wait_ge(V.h, V.v)
                mk(V, v.tensor_copy(topif[:], topiu[:]))
                mk(V, v.reduce_sum(st[:, 9:10], topv[:], axis=AX.X))
                v.wait_ge(V.h, V.v)
                mk(V, v.reciprocal(st[:, 10:11], st[:, 9:10]))
                v.wait_ge(V.h, V.v)
                mk(V, v.tensor_scalar(w8[:], topv[:], st[:, 10:11], 0.5,
                                      op0=ALU.mult, op1=ALU.mult))
                for j in range(4):
                    mk(V, v.tensor_single_scalar(ohp[:, j, :], iota32[:],
                                                 topif[:, j:j + 1], op=ALU.is_equal))
                mile[("rt_w8", sub, c)] = V.v

            for k in range(4):
                @on_v
                def _(v, k=k):
                    v.wait_ge(V.h, mile[("rt_w8", sub, c)])
                    if ("rt_ohsT_t", sub, c, k - 1) in mile:
                        v.wait_ge(TE.h, mile[("rt_ohsT_t", sub, c, k - 1)])
                    mk(V, v.tensor_single_scalar(ohb[:], iota32[:],
                                                 topif[:, 4 + k:5 + k], op=ALU.is_equal))
                    mile[("rt_ohs", sub, c, k)] = V.v

                @on_t
                def _(t, k=k):
                    t.wait_ge(V.h, mile[("rt_ohs", sub, c, k)])
                    if ("rt_G_e", sub, c, k - 1) in mile:
                        t.wait_ge(SC.h, mile[("rt_G_e", sub, c, k - 1)])
                    elif ("rt_G_e", sub, c - 1, 3) in mile:
                        t.wait_ge(SC.h, mile[("rt_G_e", sub, c - 1, 3)])
                    mile[("rt_ohsT_t", sub, c, k)] = mk(TE, t.transpose(
                        sm_a, ohb[:, 0:32], ident_f[:]))

                @on_s
                def _(s, k=k):
                    s.wait_ge(TE.h, mile[("rt_ohsT_t", sub, c, k)])
                    mile[("rt_ohsT_e", sub, c, k)] = mk(SC, s.copy(ohsT[:, k, :], sm_a))

                @on_t
                def _(t, k=k):
                    t.wait_ge(SC.h, mile[("rt_ohsT_e", sub, c, k)])
                    t.wait_ge(V.h, mile["simdiag"])
                    mile[("rt_R_t", sub, c, k)] = mk(TE, t.matmul(
                        sm_b, simd_sb[:], ohsT[:, k, :], start=True, stop=True))

                @on_s
                def _(s, k=k):
                    s.wait_ge(TE.h, mile[("rt_R_t", sub, c, k)])
                    mile[("rt_R_e", sub, c, k)] = mk(SC, s.copy(Rsb[:, k, :], sm_b))

                @on_t
                def _(t, k=k):
                    t.wait_ge(SC.h, mile[("rt_R_e", sub, c, k)])
                    mile[("rt_G_t", sub, c, k)] = mk(TE, t.transpose(
                        sm_c, Rsb[:, k, :], ident32[:]))

                @on_s
                def _(s, k=k):
                    s.wait_ge(TE.h, mile[("rt_G_t", sub, c, k)])
                    mile[("rt_G_e", sub, c, k)] = mk(SC, s.copy(Gk[:, k, :], sm_c))

            @on_v
            def _(v):
                for k in range(4):
                    v.wait_ge(SC.h, mile[("rt_G_e", sub, c, k)])
                    for j in range(4):
                        mk(V, v.tensor_mul(tmp32[:], Gk[:, k, :], ohp[:, j, :]))
                        v.wait_ge(V.h, V.v)
                        mk(V, v.reduce_sum(scand[:, 4 * k + j:4 * k + j + 1],
                                           tmp32[:], axis=AX.X))
                        v.wait_ge(V.h, V.v)
                mk(V, v.memset(cw[:, c, :], 0.0))
                v.wait_ge(V.h, V.v)
                for j in range(4):
                    mk(V, v.tensor_single_scalar(ohb[:], iota32[:], topif[:, j:j + 1],
                                                 op=ALU.is_equal))
                    v.wait_ge(V.h, V.v)
                    mk(V, v.scalar_tensor_tensor(cw[:, c, :], ohb[:], w8[:, j:j + 1],
                                                 cw[:, c, :], op0=ALU.mult, op1=ALU.add))
                    v.wait_ge(V.h, V.v)
                for k in range(4):
                    sl = scand[:, 4 * k:4 * k + 4]
                    mk(V, v.reduce_max(st[:, 11:12], sl, axis=AX.X))
                    v.wait_ge(V.h, V.v)
                    mk(V, v.tensor_single_scalar(tmpk[:, 0:4], sl, st[:, 11:12],
                                                 op=ALU.is_ge))
                    v.wait_ge(V.h, V.v)
                    mk(V, v.tensor_mul(tmpk[:, 4:8], tmpk[:, 0:4], iota4m[:]))
                    v.wait_ge(V.h, V.v)
                    mk(V, v.tensor_reduce(st[:, 12:13], tmpk[:, 4:8], axis=AX.X,
                                          op=ALU.min))
                    v.wait_ge(V.h, V.v)
                    mk(V, v.tensor_single_scalar(tmpk[:, 0:4], iota4m[:], st[:, 12:13],
                                                 op=ALU.is_equal))
                    v.wait_ge(V.h, V.v)
                    mk(V, v.tensor_mul(tmpk[:, 4:8], tmpk[:, 0:4], topif[:, 0:4]))
                    v.wait_ge(V.h, V.v)
                    mk(V, v.reduce_sum(st[:, 13:14], tmpk[:, 4:8], axis=AX.X))
                    v.wait_ge(V.h, V.v)
                    mk(V, v.tensor_single_scalar(st[:, 14:15], st[:, 11:12], 0.5,
                                                 op=ALU.is_lt))
                    v.wait_ge(V.h, V.v)
                    mk(V, v.tensor_mul(st[:, 15:16], st[:, 14:15],
                                       topif[:, 4 + k:5 + k]))
                    v.wait_ge(V.h, V.v)
                    mk(V, v.tensor_scalar(st[:, 16:17], st[:, 14:15], -1.0, 1.0,
                                          op0=ALU.mult, op1=ALU.add))
                    v.wait_ge(V.h, V.v)
                    mk(V, v.tensor_mul(st[:, 16:17], st[:, 16:17], st[:, 13:14]))
                    v.wait_ge(V.h, V.v)
                    mk(V, v.tensor_add(st[:, 15:16], st[:, 15:16], st[:, 16:17]))
                    v.wait_ge(V.h, V.v)
                    mk(V, v.tensor_single_scalar(ohb[:], iota32[:], st[:, 15:16],
                                                 op=ALU.is_equal))
                    v.wait_ge(V.h, V.v)
                    mk(V, v.scalar_tensor_tensor(cw[:, c, :], ohb[:],
                                                 w8[:, 4 + k:5 + k], cw[:, c, :],
                                                 op0=ALU.mult, op1=ALU.add))
                    v.wait_ge(V.h, V.v)
                mile[("rt_done", sub, c)] = V.v

        ffn_gidx = [0]
        ffn_eidx = [0]

        def emit_ffn(sub):
            for c in range(NTC):
                emit_ln_chunk(c, sub)
                emit_router_chunk(c, sub)
            rt_all = mile[("rt_done", sub, NTC - 1)]
            zt_all = mile[("zT_done", sub, NTC - 1)]
            sq_all = mile[("ln_ssq", sub, NTC - 1)]
            rtsc_all = SC.v

            for e in range(NE):
                einst = ffn_eidx[0]
                ffn_eidx[0] += 1

                for h in range(2):
                    gidx = ffn_gidx[0]
                    ffn_gidx[0] += 1
                    par = gidx % 2

                    @on_y
                    def _(y, e=e, h=h, gidx=gidx, par=par):
                        if sub == 0:
                            y.wait_ge(LG.h, mile[("cast", e)])
                        if ("mm1_half", gidx - 2) in mile:
                            y.wait_ge(TE.h, mile[("mm1_half", gidx - 2)])
                        for dc in range(NDC):
                            y.dma_start_transpose(
                                wgT[:, par, dc, 0:384],
                                gup_bf[e, 384 * h:384 * (h + 1),
                                       dc * 128:(dc + 1) * 128],
                            ).then_inc(LY.h, 16)
                            y.dma_start_transpose(
                                wgT[:, par, dc, 384:768],
                                gup_bf[e, 768 + 384 * h:768 + 384 * (h + 1),
                                       dc * 128:(dc + 1) * 128],
                            ).then_inc(LY.h, 16)
                            LY.v += 32
                        mile[("wg_ld", gidx)] = LY.v

                    @on_t
                    def _(t, e=e, h=h, gidx=gidx, par=par):
                        t.wait_ge(LY.h, mile[("wg_ld", gidx)])
                        if h == 0 and e == 0:
                            t.wait_ge(SC.h, sq_all)
                            t.wait_ge(SC.h, zt_all)
                            t.wait_ge(V.h, rt_all)
                        for fcw in range(3):
                            if ("mm1_silu", gidx, fcw - 1) in mile:
                                t.wait_ge(SC.h, mile[("mm1_silu", gidx, fcw - 1)])
                            elif fcw == 0 and ("mm1_silu", gidx - 1, 2) in mile:
                                t.wait_ge(SC.h, mile[("mm1_silu", gidx - 1, 2)])
                            mm = None
                            for dc in range(NDC):
                                for th in range(2):
                                    mm = t.matmul(
                                        PS1[:, th * 512:(th + 1) * 512],
                                        wgT[:, par, dc, fcw * 128:(fcw + 1) * 128],
                                        zT[:, dc, th * 512:(th + 1) * 512],
                                        start=(dc == 0), stop=(dc == NDC - 1))
                            mile[("mm1_g", gidx, fcw)] = mk(TE, mm)
                            if ("mm1_hm", gidx, fcw - 1) in mile:
                                t.wait_ge(V.h, mile[("mm1_hm", gidx, fcw - 1)])
                            elif fcw == 0 and ("mm1_hm", gidx - 1, 2) in mile:
                                t.wait_ge(V.h, mile[("mm1_hm", gidx - 1, 2)])
                            mm = None
                            for dc in range(NDC):
                                for th in range(2):
                                    mm = t.matmul(
                                        PS1[:, (2 + th) * 512:(3 + th) * 512],
                                        wgT[:, par, dc,
                                            384 + fcw * 128:384 + (fcw + 1) * 128],
                                        zT[:, dc, th * 512:(th + 1) * 512],
                                        start=(dc == 0), stop=(dc == NDC - 1))
                            mile[("mm1_u", gidx, fcw)] = mk(TE, mm)
                        mile[("mm1_half", gidx)] = TE.v

                    @on_s
                    def _(s, gidx=gidx):
                        for fcw in range(3):
                            s.wait_ge(TE.h, mile[("mm1_g", gidx, fcw)])
                            if ("mm1_hm", gidx, fcw - 1) in mile:
                                s.wait_ge(V.h, mile[("mm1_hm", gidx, fcw - 1)])
                            elif fcw == 0 and ("mm1_hm", gidx - 1, 2) in mile:
                                s.wait_ge(V.h, mile[("mm1_hm", gidx - 1, 2)])
                            s.activation(silu_t[:, 0, :], PS1[:, 0:512], AF.Silu)
                            mm = s.activation(silu_t[:, 1, :], PS1[:, 512:1024],
                                              AF.Silu)
                            mile[("mm1_silu", gidx, fcw)] = mk(SC, mm)

                    @on_v
                    def _(v, gidx=gidx, h=h):
                        for fcw in range(3):
                            fc = h * 3 + fcw
                            v.wait_ge(TE.h, mile[("mm1_u", gidx, fcw)])
                            v.wait_ge(SC.h, mile[("mm1_silu", gidx, fcw)])
                            mm = None
                            for th in range(2):
                                mm = v.tensor_mul(
                                    hT[:, fc, th * 512:(th + 1) * 512],
                                    silu_t[:, th, :],
                                    PS1[:, (2 + th) * 512:(3 + th) * 512])
                            mile[("mm1_hm", gidx, fcw)] = mk(V, mm)

                @on_y
                def _(y, e=e, einst=einst):
                    if sub == 0:
                        y.wait_ge(LG.h, mile[("cast", e)])
                    if ("mm2_done", einst - 1) in mile:
                        y.wait_ge(TE.h, mile[("mm2_done", einst - 1)])
                    for fc in range(6):
                        y.dma_start_transpose(
                            wdT[:, fc, :], dwn_bf[e, :, fc * 128:(fc + 1) * 128],
                        ).then_inc(LY.h, 16)
                        LY.v += 16
                    mile[("wd_ld", einst)] = LY.v

                @on_t
                def _(t, e=e, einst=einst):
                    t.wait_ge(LY.h, mile[("wd_ld", einst)])
                    t.wait_ge(V.h, mile[("mm1_hm", 2 * einst + 1, 2)])
                    if e == 0:
                        t.wait_ge(V.h, rt_all)
                        t.wait_ge(SC.h, rtsc_all)
                    for tc in range(NTC):
                        for dgp in range(2):
                            u = tc * 2 + dgp
                            bk = (u % 2) * 2
                            if ("mm2_ev", einst, u - 2) in mile:
                                t.wait_ge(V.h, mile[("mm2_ev", einst, u - 2)])
                            elif u < 2 and ("mm2_ev", einst - 1, 14 + u) in mile:
                                t.wait_ge(V.h, mile[("mm2_ev", einst - 1, 14 + u)])
                            mm = None
                            for fc in range(6):
                                for dgi in range(2):
                                    mm = t.matmul(
                                        PS2[:, (bk + dgi) * 512:(bk + dgi + 1) * 512],
                                        hT[:, fc, tc * 128:(tc + 1) * 128],
                                        wdT[:, fc, (dgp * 2 + dgi) * 512:
                                            (dgp * 2 + dgi + 1) * 512],
                                        start=(fc == 0), stop=(fc == 5))
                            mile[("mm2_u", einst, u)] = mk(TE, mm)
                    mile[("mm2_done", einst)] = TE.v

                @on_v
                def _(v, e=e, einst=einst):
                    for tc in range(NTC):
                        for dgp in range(2):
                            u = tc * 2 + dgp
                            bk = (u % 2) * 2
                            v.wait_ge(TE.h, mile[("mm2_u", einst, u)])
                            mm = None
                            for dgi in range(2):
                                dg = dgp * 2 + dgi
                                mm = v.scalar_tensor_tensor(
                                    x_sb[:, tc, dg * 512:(dg + 1) * 512],
                                    PS2[:, (bk + dgi) * 512:(bk + dgi + 1) * 512],
                                    cw[:, tc, e:e + 1],
                                    x_sb[:, tc, dg * 512:(dg + 1) * 512],
                                    op0=ALU.mult, op1=ALU.add)
                            mile[("mm2_ev", einst, u)] = mk(V, mm)
            mile[("ffn_done", sub)] = V.v

        def emit_attn():
            sub = 1
            for c in range(NTC):
                emit_ln_chunk(c, sub)
            zt_all = mile[("zT_done", sub, NTC - 1)]
            sq_all = mile[("ln_ssq", sub, NTC - 1)]

            @on_g
            def _(g):
                g.wait_ge(LG.h, mile["attw"])
                g.dma_start(out=wk_v[:], in_=wkbf).then_inc(LG.h, 16); LG.v += 16
                g.dma_start(out=wvv[:], in_=wvbf).then_inc(LG.h, 16); LG.v += 16
                mile["wkv_cast"] = LG.v

            for tc in range(NTC):
                @on_t
                def _(t, tc=tc):
                    if tc == 0:
                        t.wait_ge(LG.h, mile["wkv_cast"])
                        t.wait_ge(SC.h, zt_all)
                        t.wait_ge(SC.h, sq_all)
                    ko = 1024 if tc % 2 == 0 else 0
                    vo = 1536 if tc % 2 == 0 else 512
                    if ("kv_e", tc - 2) in mile:
                        t.wait_ge(SC.h, mile[("kv_e", tc - 2)])
                    mm = None
                    for dc in range(NDC):
                        t.matmul(PS2[:, ko:ko + 128],
                                 zT[:, dc, tc * 128:(tc + 1) * 128],
                                 wk_v[:, dc, :], start=(dc == 0), stop=(dc == NDC - 1))
                        mm = t.matmul(PS2[:, vo:vo + 128],
                                      zT[:, dc, tc * 128:(tc + 1) * 128],
                                      wvv[:, dc, :], start=(dc == 0),
                                      stop=(dc == NDC - 1))
                    mile[("kv_t", tc)] = mk(TE, mm)

                @on_s
                def _(s, tc=tc):
                    ko = 1024 if tc % 2 == 0 else 0
                    vo = 1536 if tc % 2 == 0 else 512
                    s.wait_ge(TE.h, mile[("kv_t", tc)])
                    s.copy(kv_sb[:, tc, 0:128], PS2[:, ko:ko + 128])
                    mile[("kv_e", tc)] = mk(SC, s.copy(kv_sb[:, tc, 128:256],
                                                       PS2[:, vo:vo + 128]))

            @on_g
            def _(g):
                g.wait_ge(SC.h, mile[("kv_e", NTC - 1)])
                g.dma_start(out=kvl3, in_=kv_sb[:]).then_inc(LG.h, 16); LG.v += 16
                g.wait_ge(LG.h, LG.v)
                cc = g.collective_compute(
                    "AllGather", ALU.bypass,
                    replica_groups=[[0, 1], [2, 3], [4, 5], [6, 7]],
                    ins=[kv_loc.opt()], outs=[kv_all.opt()])
                cc.then_inc(CC.h)
                CC.v += 1
                mile["cc"] = CC.v
                g.dma_start(out=wq_v[:], in_=wqbf[:, :, 0:1024]).then_inc(LG.h, 16)
                LG.v += 16
                mile[("wq_cast", 0)] = LG.v
                g.wait_ge(CC.h, mile["cc"])
                g.wait_ge(TE.h, mile[("kv_t", NTC - 1)])
                g.dma_start(out=v_sb[:], in_=kva_v[:, :, 128:256]).then_inc(LG.h, 16)
                LG.v += 16
                mile["v_ld"] = LG.v

            @on_y
            def _(y):
                y.wait_ge(CC.h, mile["cc"])
                y.wait_ge(TE.h, mile[("kv_t", NTC - 1)])
                y.dma_start_transpose(kT_v, kv_all[:, 0:128]).then_inc(LY.h, 16)
                LY.v += 16
                mile["kT_ld"] = LY.v

            for hg in range(2):
                if hg == 1:
                    @on_g
                    def _(g):
                        g.wait_ge(TE.h, mile[("op_t", 0, 15)])
                        g.dma_start(out=wq_v[:],
                                    in_=wqbf[:, :, 1024:2048]).then_inc(LG.h, 16)
                        LG.v += 16
                        mile[("wq_cast", 1)] = LG.v
                for hc in range(8):
                    @on_t
                    def _(t, hg=hg, hc=hc):
                        if hc == 0:
                            t.wait_ge(LG.h, mile[("wq_cast", hg)])
                            if hg == 0:
                                t.wait_ge(SC.h, mile[("kv_e", NTC - 1)])
                        bk = (hc % 2) * 2
                        if ("q_e", hg, hc - 2) in mile:
                            t.wait_ge(SC.h, mile[("q_e", hg, hc - 2)])
                        elif hg == 1 and hc < 2:
                            t.wait_ge(V.h, mile[("op_ev", 0, 14 + hc)])
                            t.wait_ge(SC.h, mile[("q_e", 0, 6 + hc)])
                        mm = None
                        for dc in range(NDC):
                            for th in range(2):
                                mm = t.matmul(
                                    PS2[:, (bk + th) * 512:(bk + th + 1) * 512],
                                    wq_v[:, dc, hc * 128:(hc + 1) * 128],
                                    zT[:, dc, th * 512:(th + 1) * 512],
                                    start=(dc == 0), stop=(dc == NDC - 1))
                        mile[("q_t", hg, hc)] = mk(TE, mm)

                    @on_s
                    def _(s, hg=hg, hc=hc):
                        bk = (hc % 2) * 2
                        s.wait_ge(TE.h, mile[("q_t", hg, hc)])
                        mm = None
                        for th in range(2):
                            mm = s.activation(qT[:, hc, th * 512:(th + 1) * 512],
                                              PS2[:, (bk + th) * 512:(bk + th + 1) * 512],
                                              AF.Copy, scale=RSQ)
                        mile[("q_e", hg, hc)] = mk(SC, mm)

                @on_g
                def _(g, hg=hg):
                    g.wait_ge(TE.h, mile[("q_t", hg, 7)])
                    g.dma_start(out=wo_v[:], in_=wobf[hg]).then_inc(LG.h, 16)
                    LG.v += 16
                    mile[("wo_cast", hg)] = LG.v

                for hc in range(8):
                    for gq in range(2):
                        nblk = 4 * gq + 4

                        @on_v
                        def _(v, hg=hg, hc=hc, gq=gq, nblk=nblk):
                            if ("av_t", hg, hc, gq - 1) in mile:
                                v.wait_ge(TE.h, mile[("av_t", hg, hc, gq - 1)])
                            elif ("av_t", hg, hc - 1, 1) in mile:
                                v.wait_ge(TE.h, mile[("av_t", hg, hc - 1, 1)])
                            elif hg == 1 and ("av_t", 0, 7, 1) in mile:
                                v.wait_ge(TE.h, mile[("av_t", 0, 7, 1)])
                            mm = None
                            for tql in range(4):
                                ext = 4 * gq + tql + 1
                                for b in range(ext, nblk):
                                    mm = v.memset(AT[:, b, tql * 128:(tql + 1) * 128], 0.0)
                                    mm = v.memset(AT[:, 8 + b, tql * 128:(tql + 1) * 128],
                                                  0.0)
                            if mm is not None:
                                mile[("at_ms", hg, hc, gq)] = mk(V, mm)
                            else:
                                mile[("at_ms", hg, hc, gq)] = V.v

                        for tql in range(4):
                            tqc = 4 * gq + tql
                            ext = tqc + 1

                            @on_t
                            def _(t, hg=hg, hc=hc, gq=gq, tql=tql, tqc=tqc, ext=ext):
                                for key in [("sm_exp", hg, hc, gq, tql - 1),
                                            ("sm_exp", hg, hc, gq - 1, 3),
                                            ("sm_exp", hg, hc - 1, 1, 3),
                                            ("sm_exp", hg - 1, 7, 1, 3)]:
                                    if key in mile:
                                        t.wait_ge(SC.h, mile[key])
                                        break
                                if hc == 0 and gq == 0 and tql == 0:
                                    t.wait_ge(LY.h, mile["kT_ld"])
                                    if hg == 0:
                                        t.wait_ge(SC.h, sq_all)
                                t.wait_ge(SC.h, mile[("q_e", hg, hc)])
                                mm = None
                                for half in range(2):
                                    pb = half * 1024
                                    n512 = (ext - 1) // 4
                                    for i5 in range(n512):
                                        mm = t.matmul(
                                            PS1[:, pb + i5 * 512:pb + (i5 + 1) * 512],
                                            qT[:, hc, tqc * 128:(tqc + 1) * 128],
                                            kT_v[:, pb + i5 * 512:pb + (i5 + 1) * 512],
                                            start=True, stop=True)
                                    for b in range(n512 * 4, ext):
                                        mm = t.matmul(
                                            PS1[:, pb + b * 128:pb + (b + 1) * 128],
                                            qT[:, hc, tqc * 128:(tqc + 1) * 128],
                                            kT_v[:, pb + b * 128:pb + (b + 1) * 128],
                                            start=True, stop=True)
                                mile[("sc_t", hg, hc, gq, tql)] = mk(TE, mm)

                            @on_v
                            def _(v, hg=hg, hc=hc, gq=gq, tql=tql, ext=ext):
                                v.wait_ge(TE.h, mile[("sc_t", hg, hc, gq, tql)])
                                d = (ext - 1) * 128
                                mk(V, v.tensor_add(PS1[:, d:d + 128],
                                                   PS1[:, d:d + 128], mke[:]))
                                mk(V, v.tensor_add(PS1[:, 1024 + d:1024 + d + 128],
                                                   PS1[:, 1024 + d:1024 + d + 128],
                                                   mko[:]))
                                v.wait_ge(V.h, V.v)
                                mk(V, v.reduce_max(st[:, 20:21], PS1[:, 0:ext * 128],
                                                   axis=AX.X))
                                mk(V, v.reduce_max(st[:, 21:22],
                                                   PS1[:, 1024:1024 + ext * 128],
                                                   axis=AX.X))
                                v.wait_ge(V.h, V.v)
                                mk(V, v.tensor_max(st[:, 22:23], st[:, 20:21],
                                                   st[:, 21:22]))
                                v.wait_ge(V.h, V.v)
                                mk(V, v.tensor_scalar_mul(st[:, 23:24], st[:, 22:23],
                                                          -1.0))
                                mile[("sm_nm", hg, hc, gq, tql)] = V.v

                            @on_s
                            def _(s, hg=hg, hc=hc, gq=gq, tql=tql, ext=ext):
                                s.wait_ge(V.h, mile[("sm_nm", hg, hc, gq, tql)])
                                s.activation(A1[:, 0:ext * 128], PS1[:, 0:ext * 128],
                                             AF.Exp, bias=st[:, 23:24],
                                             accum_out=st[:, 24:25])
                                mm = s.activation(A1[:, 1024:1024 + ext * 128],
                                                  PS1[:, 1024:1024 + ext * 128],
                                                  AF.Exp, bias=st[:, 23:24],
                                                  accum_out=st[:, 25:26])
                                mile[("sm_exp", hg, hc, gq, tql)] = mk(SC, mm)

                            @on_v
                            def _(v, hg=hg, hc=hc, gq=gq, tql=tql, ext=ext):
                                v.wait_ge(SC.h, mile[("sm_exp", hg, hc, gq, tql)])
                                for key in [("at_ld", hg, hc, gq, tql - 1),
                                            ("at_ld", hg, hc, gq - 1, 3),
                                            ("at_ld", hg, hc - 1, 1, 3),
                                            ("at_ld", hg - 1, 7, 1, 3)]:
                                    if key in mile:
                                        v.wait_ge(LY.h, mile[key])
                                        break
                                mk(V, v.tensor_add(st[:, 26:27], st[:, 24:25],
                                                   st[:, 25:26]))
                                v.wait_ge(V.h, V.v)
                                mk(V, v.reciprocal(st[:, 27:28], st[:, 26:27]))
                                v.wait_ge(V.h, V.v)
                                mk(V, v.tensor_scalar_mul(A2[:, 0:ext * 128],
                                                          A1[:, 0:ext * 128],
                                                          st[:, 27:28]))
                                mm = v.tensor_scalar_mul(A2[:, 1024:1024 + ext * 128],
                                                         A1[:, 1024:1024 + ext * 128],
                                                         st[:, 27:28])
                                mile[("sm_n", hg, hc, gq, tql)] = mk(V, mm)

                            @on_y
                            def _(y, hg=hg, hc=hc, gq=gq, tql=tql, ext=ext):
                                y.wait_ge(V.h, mile[("sm_n", hg, hc, gq, tql)])
                                y.wait_ge(V.h, mile[("at_ms", hg, hc, gq)])
                                for b in range(ext):
                                    y.dma_start_transpose(
                                        AT[:, b, tql * 128:(tql + 1) * 128],
                                        A2[:, b * 128:(b + 1) * 128]).then_inc(LY.h, 16)
                                    y.dma_start_transpose(
                                        AT[:, 8 + b, tql * 128:(tql + 1) * 128],
                                        A2[:, 1024 + b * 128:1024 + (b + 1) * 128],
                                    ).then_inc(LY.h, 16)
                                    LY.v += 32
                                mile[("at_ld", hg, hc, gq, tql)] = LY.v

                        @on_t
                        def _(t, hg=hg, hc=hc, gq=gq, nblk=nblk):
                            t.wait_ge(LY.h, mile[("at_ld", hg, hc, gq, 3)])
                            if hc == 0 and gq == 0:
                                t.wait_ge(LG.h, mile["v_ld"])
                                t.wait_ge(SC.h, mile[("q_e", hg, 7)])
                            u = (hc * 2 + gq) % 2
                            ob = PS2[:, u * 512:(u + 1) * 512]
                            for key in [("ot_e", hg, hc, gq - 2),
                                        ("ot_e", hg, hc - 1, gq),
                                        ("ot_e", hg - 1, 7, gq)]:
                                if key in mile:
                                    t.wait_ge(SC.h, mile[key])
                                    break
                            slots = list(range(nblk)) + [8 + b for b in range(nblk)]
                            mm = None
                            for i, sl in enumerate(slots):
                                mm = t.matmul(ob, v_sb[:, sl, :], AT[:, sl, :],
                                              start=(i == 0), stop=(i == len(slots) - 1))
                            mile[("av_t", hg, hc, gq)] = mk(TE, mm)

                        @on_s
                        def _(s, hg=hg, hc=hc, gq=gq):
                            u = (hc * 2 + gq) % 2
                            ob = PS2[:, u * 512:(u + 1) * 512]
                            s.wait_ge(TE.h, mile[("av_t", hg, hc, gq)])
                            mm = s.copy(qT[:, hc, gq * 512:(gq + 1) * 512], ob)
                            mile[("ot_e", hg, hc, gq)] = mk(SC, mm)

                @on_t
                def _(t, hg=hg):
                    t.wait_ge(LG.h, mile[("wo_cast", hg)])
                    t.wait_ge(SC.h, mile[("ot_e", hg, 7, 1)])
                    for tc in range(NTC):
                        for dgp in range(2):
                            u = tc * 2 + dgp
                            bk = (u % 2) * 2
                            if ("op_ev", hg, u - 2) in mile:
                                t.wait_ge(V.h, mile[("op_ev", hg, u - 2)])
                            elif hg == 1 and u < 2:
                                t.wait_ge(V.h, mile[("op_ev", 0, 14 + u)])
                            mm = None
                            for hc in range(8):
                                for dgi in range(2):
                                    mm = t.matmul(
                                        PS2[:, (bk + dgi) * 512:(bk + dgi + 1) * 512],
                                        qT[:, hc, tc * 128:(tc + 1) * 128],
                                        wo_v[:, hc, (dgp * 2 + dgi) * 512:
                                             (dgp * 2 + dgi + 1) * 512],
                                        start=(hc == 0), stop=(hc == 7))
                            mile[("op_t", hg, u)] = mk(TE, mm)

                @on_v
                def _(v, hg=hg):
                    for tc in range(NTC):
                        for dgp in range(2):
                            u = tc * 2 + dgp
                            bk = (u % 2) * 2
                            v.wait_ge(TE.h, mile[("op_t", hg, u)])
                            mm = None
                            for dgi in range(2):
                                dg = dgp * 2 + dgi
                                mm = v.scalar_tensor_tensor(
                                    x_sb[:, tc, dg * 512:(dg + 1) * 512],
                                    PS2[:, (bk + dgi) * 512:(bk + dgi + 1) * 512],
                                    1.0,
                                    x_sb[:, tc, dg * 512:(dg + 1) * 512],
                                    op0=ALU.mult, op1=ALU.add)
                            mile[("op_ev", hg, u)] = mk(V, mm)

            @on_v
            def _(v):
                v.wait_ge(V.h, mile[("op_ev", 1, 15)])
                v.wait_ge(SC.h, mile["borep"])
                mm = None
                for tc in range(NTC):
                    mm = v.scalar_tensor_tensor(x_sb[:, tc, :], borep[:], 1.0,
                                                x_sb[:, tc, :], op0=ALU.mult,
                                                op1=ALU.add)
                mile["attn_done"] = mk(V, mm)

        emit_ffn(0)
        if DBG:
            @on_y
            def _(y):
                y.wait_ge(V.h, mile[("ffn_done", 0)])
                y.dma_start(dbg_outs[0].rearrange("(c p) d -> p c d", p=128),
                            x_sb[:]).then_inc(LY.h, 16)
                LY.v += 16
                mile["dbgst0"] = LY.v
        barrier()
        if DBG:
            @on_t
            def _(t):
                t.wait_ge(LY.h, mile["dbgst0"])
        emit_attn()
        if DBG:
            @on_y
            def _(y):
                y.wait_ge(V.h, mile["attn_done"])
                y.dma_start(dbg_outs[1].rearrange("(c p) d -> p c d", p=128),
                            x_sb[:]).then_inc(LY.h, 16)
                LY.v += 16
                mile["dbgst1"] = LY.v
        barrier()
        if DBG:
            @on_t
            def _(t):
                t.wait_ge(LY.h, mile["dbgst1"])
        emit_ffn(2)
        barrier()

        @on_y
        def _(y):
            y.dma_start(out3, x_sb[:]).then_inc(LY.h, 16)
            LY.v += 16
            y.wait_ge(LY.h, LY.v)

    return nc


_NC_CACHE = {}


def _get_nc():
    key = ("nc", DBG)
    if key not in _NC_CACHE:
        _NC_CACHE[key] = build_nc()
    return _NC_CACHE[key]


def kernel(L0_attn_x, L0_attn_Wq, L0_attn_Wk, L0_attn_Wv, L0_attn_Wo, L0_attn_bo,
           L0_ffn_gate_weight, L0_ffn_gate_up_proj, L0_ffn_down_proj,
           L0_ffn_similarity):
    x = np.ascontiguousarray(np.asarray(L0_attn_x, dtype=np.float32))
    B, T, _ = x.shape

    f = np.arange(128)[None, :]
    p = np.arange(128)[:, None]
    mask_e = np.where(f <= p, 0.0, NEG).astype(np.float32)

    common = {
        "Wq": np.ascontiguousarray(np.asarray(L0_attn_Wq, dtype=np.float32)),
        "Wk": np.ascontiguousarray(np.asarray(L0_attn_Wk, dtype=np.float32)),
        "Wv": np.ascontiguousarray(np.asarray(L0_attn_Wv, dtype=np.float32)),
        "Wo": np.ascontiguousarray(np.asarray(L0_attn_Wo, dtype=np.float32)),
        "bo": np.ascontiguousarray(np.asarray(L0_attn_bo, dtype=np.float32)[None, :]),
        "gw": np.ascontiguousarray(np.asarray(L0_ffn_gate_weight, dtype=np.float32)),
        "gup": np.ascontiguousarray(np.asarray(L0_ffn_gate_up_proj, dtype=np.float32)),
        "dwn": np.ascontiguousarray(np.asarray(L0_ffn_down_proj, dtype=np.float32)),
        "sim": np.ascontiguousarray(np.asarray(L0_ffn_similarity, dtype=np.float32)),
        "mask_ev": mask_e,
    }
    in_maps = []
    for core in range(8):
        b, h = core // 2, core % 2
        mask_o = np.where(f <= p - 1 + h, 0.0, NEG).astype(np.float32)
        m = dict(common)
        m["x_in"] = np.ascontiguousarray(x[b, h::2, :])
        m["mask_od"] = mask_o
        in_maps.append(m)

    nc = _get_nc()
    res = run_bass_kernel_spmd(nc, in_maps, core_ids=list(range(8)))

    full = np.empty((B, T, D), dtype=np.float32)
    for core in range(8):
        b, h = core // 2, core % 2
        full[b, h::2, :] = res.results[core]["out"]
    if DBG:
        kernel.dbg = [np.stack([res.results[c][f"dbg{i}"] for c in range(8)])
                      for i in range(2)]
    return full


# version 05
# speedup vs baseline: 1.1656x; 1.1656x over previous
import numpy as np
import ml_dtypes
from contextlib import ExitStack

import concourse.bass as bass
import concourse.mybir as mybir
from concourse.bass_utils import run_bass_kernel_spmd

F32 = mybir.dt.float32
BF16 = mybir.dt.bfloat16
U32 = mybir.dt.uint32
AF = mybir.ActivationFunctionType
ALU = mybir.AluOpType
AX = mybir.AxisListType

D = 2048
NT = 1024
NTC = 8
NDC = 16
NE = 32
RSQ = float(np.float32(1.0) / np.float32(np.sqrt(np.float32(128.0))))
EPS = 1e-5
NEG = -1.0e9

DBG = False


def build_nc():
    nc = bass.Bass("TRN2", target_bir_lowering=False, debug=False)

    x_in = nc.dram_tensor("x_in", [NT, D], F32, kind="ExternalInput").ap()
    Wq = nc.dram_tensor("Wq", [D, D], F32, kind="ExternalInput").ap()
    Wk = nc.dram_tensor("Wk", [D, 128], F32, kind="ExternalInput").ap()
    Wv = nc.dram_tensor("Wv", [D, 128], F32, kind="ExternalInput").ap()
    Wo = nc.dram_tensor("Wo", [D, D], F32, kind="ExternalInput").ap()
    bo = nc.dram_tensor("bo", [1, D], F32, kind="ExternalInput").ap()
    gw = nc.dram_tensor("gw", [NE, D], F32, kind="ExternalInput").ap()
    gup = nc.dram_tensor("gup", [NE, 1536, D], F32, kind="ExternalInput").ap()
    dwn = nc.dram_tensor("dwn", [NE, D, 768], F32, kind="ExternalInput").ap()
    sim = nc.dram_tensor("sim", [NE, NE], F32, kind="ExternalInput").ap()
    mask_ev = nc.dram_tensor("mask_ev", [128, 128], F32, kind="ExternalInput").ap()
    mask_od = nc.dram_tensor("mask_od", [128, 128], F32, kind="ExternalInput").ap()
    out = nc.dram_tensor("out", [NT, D], F32, kind="ExternalOutput").ap()
    dbg_outs = []
    if DBG:
        for i in range(2):
            dbg_outs.append(nc.dram_tensor(f"dbg{i}", [NT, D], F32,
                                           kind="ExternalOutput").ap())

    gup_bf = nc.dram_tensor("gup_bf", [NE, 1536, D], BF16, kind="Internal").ap()
    dwn_bf = nc.dram_tensor("dwn_bf", [NE, D, 768], BF16, kind="Internal").ap()
    wqbf = nc.dram_tensor("wqbf", [128, NDC, D], BF16, kind="Internal").ap()
    wobf = nc.dram_tensor("wobf", [2, 128, 8, D], BF16, kind="Internal").ap()
    wkbf = nc.dram_tensor("wkbf", [128, NDC, 128], BF16, kind="Internal").ap()
    wvbf = nc.dram_tensor("wvbf", [128, NDC, 128], BF16, kind="Internal").ap()
    kv_loc = nc.dram_tensor("kv_loc", [NT, 256], BF16, kind="Internal").ap()
    kv_all = nc.dram_tensor("kv_all", [2 * NT, 256], BF16, kind="Internal").ap()

    I128b = nc.inline_tensor(np.eye(128, dtype=ml_dtypes.bfloat16), name="I128b").ap()
    I128f = nc.inline_tensor(np.eye(128, dtype=np.float32), name="I128f").ap()
    I32f = nc.inline_tensor(np.eye(32, dtype=np.float32), name="I32f").ap()
    I32b = nc.inline_tensor(np.eye(32, dtype=ml_dtypes.bfloat16), name="I32b").ap()
    ones1b = nc.inline_tensor(np.ones((1, 128), dtype=ml_dtypes.bfloat16), name="ones1b").ap()

    x3 = x_in.rearrange("(c p) d -> p c d", p=128)
    out3 = out.rearrange("(c p) d -> p c d", p=128)
    gup4 = gup.rearrange("e (a r) d -> e a r d", r=384)
    kvl3 = kv_loc.rearrange("(c p) n -> p c n", p=128)
    kva_v = kv_all.rearrange("(c p) n -> p c n", p=128)
    wq_r = Wq.rearrange("(c p) d -> p c d", p=128)
    wk_r = Wk.rearrange("(c p) d -> p c d", p=128)
    wv_r = Wv.rearrange("(c p) d -> p c d", p=128)
    wo_r = Wo.rearrange("(g hc p) d -> g p hc d", g=2, p=128)

    ctx = ExitStack()
    with ctx:
        e_ = ctx.enter_context

        def sbt(name, shape, dt):
            return e_(nc.sbuf_tensor(name, shape, dt))

        x_sb = sbt("x_sb", [128, NTC, D], F32)
        zT = sbt("zT", [128, NDC, NT], BF16)
        zc = sbt("zc", [128, D], BF16)
        AR1 = sbt("AR1", [128, 24576], BF16)
        AR2 = sbt("AR2", [128, 20480], BF16)
        silu_t = sbt("silu_t", [128, 2, 512], F32)
        cw = sbt("cw", [128, NTC, NE], F32)
        gwT = sbt("gwT", [128, NDC, NE], BF16)
        borep = sbt("borep", [128, D], BF16)
        st = sbt("st", [128, 64], F32)
        exps = sbt("exps", [128, NE], F32)
        topv = sbt("topv", [128, 8], F32)
        topiu = sbt("topiu", [128, 8], U32)
        topif = sbt("topif", [128, 8], F32)
        w8 = sbt("w8", [128, 8], F32)
        ohp = sbt("ohp", [128, 4, NE], F32)
        ohb = sbt("ohb", [128, NE], F32)
        scand = sbt("scand", [128, 16], F32)
        tmp32 = sbt("tmp32", [128, NE], F32)
        tmpk = sbt("tmpk", [128, 8], F32)
        iota32 = sbt("iota32", [128, NE], F32)
        iota4m = sbt("iota4m", [128, 4], F32)
        ident_b = sbt("ident_b", [128, 128], BF16)
        ident_f = sbt("ident_f", [128, 128], F32)
        ident32 = sbt("ident32", [32, 32], F32)
        ident32b = sbt("ident32b", [32, 32], BF16)
        ones_b = sbt("ones_b", [1, 128], BF16)
        mke = sbt("mke", [128, 128], F32)
        mko = sbt("mko", [128, 128], F32)
        simd_sb = sbt("simd_sb", [32, 32], F32)
        sim_t = sbt("sim_t", [32, 32], F32)
        d32 = sbt("d32", [32, 32], F32)
        m32 = sbt("m32", [32, 32], F32)
        ohsT = sbt("ohsT", [32, 4, 128], F32)
        Rsb = sbt("Rsb", [32, 4, 128], F32)
        Gk = sbt("Gk", [128, 4, NE], F32)

        gwbf = AR1[0:32, 0:2048]
        bo_bf = AR1[0:1, 4096:6144]

        wgT = AR1[:].rearrange("p (b c f) -> p b c f", b=2, c=NDC)
        wq_v = AR1[:, 0:16384].rearrange("p (c n) -> p c n", c=NDC)
        wo_v = AR1[:, 0:16384].rearrange("p (c n) -> p c n", c=8)
        wk_v = AR1[:, 16384:18432].rearrange("p (c n) -> p c n", c=NDC)
        wvv = AR1[:, 18432:20480].rearrange("p (c n) -> p c n", c=NDC)
        kT_v = AR1[:, 16384:18432]
        v_sb = AR1[:, 18432:20480].rearrange("p (c n) -> p c n", c=NDC)
        kv_sb = AR1[:, 20480:22528].rearrange("p (c n) -> p c n", c=NTC)

        wdT = AR2[:, 0:12288].rearrange("p (c n) -> p c n", c=6)
        hT = AR2[:, 12288:18432].rearrange("p (c n) -> p c n", c=6)
        qT = AR2[:, 0:8192].rearrange("p (c n) -> p c n", c=8)
        AT = AR2[:, 8192:16384].rearrange("p (c n) -> p c n", c=16)
        A1 = AR2[:, 16384:18432]
        A2 = AR2[:, 18432:20480]

        PS1 = e_(nc.psum_tensor("PS1", [128, 2048], F32))
        PS2 = e_(nc.psum_tensor("PS2", [128, 2048], F32))

        def psbf(slot):
            return PS2[:, slot * 512:slot * 512 + 64].bitcast(BF16)

        psbf32 = PS2[:, 0:16].bitcast(BF16)

        class Ctr:
            def __init__(self, name):
                self.h = e_(nc.semaphore(name))
                self.v = 0

        TE = Ctr("TE"); V = Ctr("V"); SC = Ctr("SC"); G = Ctr("G")
        LG = Ctr("LG"); LY = Ctr("LY"); CC = Ctr("CC")

        def mk(ctr, instr, k=1):
            instr.then_inc(ctr.h, k)
            ctr.v += k
            return ctr.v

        block = e_(nc.Block())
        on_t, on_v, on_s, on_g, on_y = (block.tensor, block.vector, block.scalar,
                                        block.gpsimd, block.sync)
        mile = {}

        def barrier():
            tev, vv, scv, gv = TE.v, V.v, SC.v, G.v

            def w(x):
                x.wait_ge(TE.h, tev)
                x.wait_ge(V.h, vv)
                x.wait_ge(SC.h, scv)
                x.wait_ge(G.h, gv)
            on_t(w); on_v(w); on_s(w); on_g(w); on_y(w)

        @on_y
        def _(y):
            y.dma_start(out=x_sb[:], in_=x3).then_inc(LY.h, 16); LY.v += 16
            for src, dst in [(I128b, ident_b[:]), (I128f, ident_f[:]),
                             (I32f, ident32[:]), (I32b, ident32b[:]),
                             (ones1b, ones_b[:]),
                             (mask_ev, mke[:]), (mask_od, mko[:]),
                             (sim, sim_t[:])]:
                y.dma_start(out=dst, in_=src).then_inc(LY.h, 16); LY.v += 16
            mile["p0_loads"] = LY.v

        @on_g
        def _(g):
            mk(G, g.iota(iota32[:], pattern=[[1, NE]], base=0, channel_multiplier=0,
                         allow_small_or_imprecise_dtypes=True))
            mk(G, g.iota(iota4m[:], pattern=[[1, 4]], base=-100, channel_multiplier=0,
                         allow_small_or_imprecise_dtypes=True))
            mk(G, g.iota(d32[:], pattern=[[1, 32]], base=0, channel_multiplier=-1,
                         allow_small_or_imprecise_dtypes=True))
            g.dma_start(out=gwbf, in_=gw).then_inc(LG.h, 16); LG.v += 16
            g.dma_start(out=bo_bf, in_=bo).then_inc(LG.h, 16); LG.v += 16
            mile["gw_cast"] = LG.v
            g.dma_start(out=wqbf, in_=wq_r).then_inc(LG.h, 16); LG.v += 16
            g.dma_start(out=wobf[0], in_=wo_r[0]).then_inc(LG.h, 16); LG.v += 16
            g.dma_start(out=wobf[1], in_=wo_r[1]).then_inc(LG.h, 16); LG.v += 16
            g.dma_start(out=wkbf, in_=wk_r).then_inc(LG.h, 16); LG.v += 16
            g.dma_start(out=wvbf, in_=wv_r).then_inc(LG.h, 16); LG.v += 16
            mile["attw"] = LG.v
            for e in range(2):
                g.dma_start(out=gup_bf[e], in_=gup[e]).then_inc(LG.h, 16)
                g.dma_start(out=dwn_bf[e], in_=dwn[e]).then_inc(LG.h, 16)
                LG.v += 32
                mile[("cast", e)] = LG.v


        @on_v
        def _(v):
            v.wait_ge(G.h, 3)
            v.wait_ge(LY.h, mile["p0_loads"])
            mk(V, v.tensor_single_scalar(m32[:], d32[:], 0.0, op=ALU.is_equal))
            v.wait_ge(V.h, V.v)
            mk(V, v.tensor_scalar(d32[:], m32[:], -1.0, 1.0, op0=ALU.mult, op1=ALU.add))
            v.wait_ge(V.h, V.v)
            mk(V, v.tensor_mul(simd_sb[:], sim_t[:], d32[:]))
            v.wait_ge(V.h, V.v)
            mk(V, v.tensor_add(simd_sb[:], simd_sb[:], m32[:]))
            mile["simdiag"] = V.v

        for dc in range(NDC):
            @on_t
            def _(t, dc=dc):
                if dc == 0:
                    t.wait_ge(LG.h, mile["gw_cast"])
                    t.wait_ge(LY.h, mile["p0_loads"])
                else:
                    t.wait_ge(SC.h, mile[("gwT_e", dc - 1)])
                mile[("gwT_t", dc)] = mk(TE, t.transpose(
                    psbf32, gwbf[:, dc * 128:(dc + 1) * 128], ident32b[:]))

            @on_s
            def _(s, dc=dc):
                s.wait_ge(TE.h, mile[("gwT_t", dc)])
                mile[("gwT_e", dc)] = mk(SC, s.copy(gwT[:, dc, :], psbf32))
        mile["gwT"] = SC.v

        @on_t
        def _(t):
            t.wait_ge(LY.h, mile["p0_loads"])
            t.wait_ge(LG.h, mile["gw_cast"])
            mm = None
            for j in range(4):
                mm = t.matmul(PS1[:, j * 512:(j + 1) * 512], ones_b[:],
                              bo_bf[:, j * 512:(j + 1) * 512], start=True, stop=True)
            mile["borep_mm"] = mk(TE, mm)

        @on_s
        def _(s):
            s.wait_ge(TE.h, mile["borep_mm"])
            mile["borep"] = mk(SC, s.copy(borep[:], PS1[:, 0:2048]))

        barrier()

        def emit_ln_chunk(c, sub):
            @on_v
            def _(v):
                mk(V, v.reduce_sum(st[:, 0:1], x_sb[:, c, :], axis=AX.X))
                mile[("ln_sum", sub, c)] = V.v

            @on_s
            def _(s):
                s.wait_ge(V.h, mile[("ln_sum", sub, c)])
                mk(SC, s.activation(PS1[:, 0:2048], x_sb[:, c, :], AF.Square,
                                    accum_out=st[:, 1:2]))
                mile[("ln_ssq", sub, c)] = SC.v

            @on_v
            def _(v):
                v.wait_ge(SC.h, mile[("ln_ssq", sub, c)])
                mk(V, v.tensor_scalar_mul(st[:, 2:3], st[:, 0:1], -1.0 / D))
                v.wait_ge(V.h, V.v)
                mk(V, v.tensor_mul(st[:, 3:4], st[:, 2:3], st[:, 2:3]))
                v.wait_ge(V.h, V.v)
                mk(V, v.tensor_scalar_mul(st[:, 1:2], st[:, 1:2], 1.0 / D))
                v.wait_ge(V.h, V.v)
                mk(V, v.tensor_sub(st[:, 3:4], st[:, 1:2], st[:, 3:4]))
                v.wait_ge(V.h, V.v)
                mk(V, v.tensor_scalar_add(st[:, 3:4], st[:, 3:4], EPS))
                mile[("ln_var", sub, c)] = V.v

            @on_s
            def _(s):
                s.wait_ge(V.h, mile[("ln_var", sub, c)])
                mk(SC, s.activation(st[:, 4:5], st[:, 3:4], AF.Sqrt))
                mile[("ln_sd", sub, c)] = SC.v

            @on_v
            def _(v):
                v.wait_ge(SC.h, mile[("ln_sd", sub, c)])
                mk(V, v.reciprocal(st[:, 5:6], st[:, 4:5]))
                v.wait_ge(V.h, V.v)
                mk(V, v.tensor_mul(st[:, 6:7], st[:, 2:3], st[:, 5:6]))
                v.wait_ge(V.h, V.v)
                if c > 0:
                    v.wait_ge(TE.h, mile[("zT_t", sub, c - 1, NDC - 1)])
                mk(V, v.tensor_scalar(zc[:], x_sb[:, c, :], st[:, 5:6], st[:, 6:7],
                                      op0=ALU.mult, op1=ALU.add))
                mile[("ln_z", sub, c)] = V.v

            for dc in range(NDC):
                @on_t
                def _(t, dc=dc):
                    if dc == 0:
                        t.wait_ge(V.h, mile[("ln_z", sub, c)])
                    slot = dc % 2
                    if ("zT_e", sub, c, dc - 2) in mile:
                        t.wait_ge(SC.h, mile[("zT_e", sub, c, dc - 2)])
                    elif c > 0 and dc < 2:
                        t.wait_ge(SC.h, mile[("zT_e", sub, c - 1, NDC - 2 + dc)])
                    mile[("zT_t", sub, c, dc)] = mk(TE, t.transpose(
                        psbf(slot), zc[:, dc * 128:(dc + 1) * 128], ident_b[:]))

                @on_s
                def _(s, dc=dc):
                    slot = dc % 2
                    s.wait_ge(TE.h, mile[("zT_t", sub, c, dc)])
                    mile[("zT_e", sub, c, dc)] = mk(SC, s.copy(
                        zT[:, dc, c * 128:(c + 1) * 128], psbf(slot)))
            mile[("zT_done", sub, c)] = SC.v

        def emit_router_chunk(c, sub):
            lgts = PS2[:, 1024:1056]
            sm_a = PS2[0:32, 1536:1664]
            sm_b = PS2[0:32, 1664:1792]
            sm_c = PS2[:, 1792:1824]

            @on_t
            def _(t):
                t.wait_ge(SC.h, mile[("zT_done", sub, c)])
                if ("rt_lg_read", sub, c - 1) in mile:
                    t.wait_ge(SC.h, mile[("rt_lg_read", sub, c - 1)])
                mm = None
                for dc in range(NDC):
                    mm = t.matmul(lgts, zT[:, dc, c * 128:(c + 1) * 128],
                                  gwT[:, dc, :], start=(dc == 0), stop=(dc == NDC - 1))
                mile[("rt_lg", sub, c)] = mk(TE, mm)

            @on_v
            def _(v):
                v.wait_ge(TE.h, mile[("rt_lg", sub, c)])
                mk(V, v.tensor_reduce(st[:, 8:9], lgts, axis=AX.X, op=ALU.max,
                                      negate=True))
                mile[("rt_nm", sub, c)] = V.v

            @on_s
            def _(s):
                s.wait_ge(V.h, mile[("rt_nm", sub, c)])
                mk(SC, s.activation(exps[:], lgts, AF.Exp, bias=st[:, 8:9]))
                mile[("rt_lg_read", sub, c)] = SC.v
                mile[("rt_exp", sub, c)] = SC.v

            @on_v
            def _(v):
                v.wait_ge(SC.h, mile[("rt_exp", sub, c)])
                mk(V, v.max(out=topv[:], in_=exps[:]))
                v.wait_ge(V.h, V.v)
                mk(V, v.max_index(out=topiu[:], in_max=topv[:], in_values=exps[:]))
                v.wait_ge(V.h, V.v)
                mk(V, v.tensor_copy(topif[:], topiu[:]))
                mk(V, v.reduce_sum(st[:, 9:10], topv[:], axis=AX.X))
                v.wait_ge(V.h, V.v)
                mk(V, v.reciprocal(st[:, 10:11], st[:, 9:10]))
                v.wait_ge(V.h, V.v)
                mk(V, v.tensor_scalar(w8[:], topv[:], st[:, 10:11], 0.5,
                                      op0=ALU.mult, op1=ALU.mult))
                for j in range(4):
                    mk(V, v.tensor_single_scalar(ohp[:, j, :], iota32[:],
                                                 topif[:, j:j + 1], op=ALU.is_equal))
                mile[("rt_w8", sub, c)] = V.v

            for k in range(4):
                @on_v
                def _(v, k=k):
                    v.wait_ge(V.h, mile[("rt_w8", sub, c)])
                    if ("rt_ohsT_t", sub, c, k - 1) in mile:
                        v.wait_ge(TE.h, mile[("rt_ohsT_t", sub, c, k - 1)])
                    mk(V, v.tensor_single_scalar(ohb[:], iota32[:],
                                                 topif[:, 4 + k:5 + k], op=ALU.is_equal))
                    mile[("rt_ohs", sub, c, k)] = V.v

                @on_t
                def _(t, k=k):
                    t.wait_ge(V.h, mile[("rt_ohs", sub, c, k)])
                    if ("rt_G_e", sub, c, k - 1) in mile:
                        t.wait_ge(SC.h, mile[("rt_G_e", sub, c, k - 1)])
                    elif ("rt_G_e", sub, c - 1, 3) in mile:
                        t.wait_ge(SC.h, mile[("rt_G_e", sub, c - 1, 3)])
                    mile[("rt_ohsT_t", sub, c, k)] = mk(TE, t.transpose(
                        sm_a, ohb[:, 0:32], ident_f[:]))

                @on_s
                def _(s, k=k):
                    s.wait_ge(TE.h, mile[("rt_ohsT_t", sub, c, k)])
                    mile[("rt_ohsT_e", sub, c, k)] = mk(SC, s.copy(ohsT[:, k, :], sm_a))

                @on_t
                def _(t, k=k):
                    t.wait_ge(SC.h, mile[("rt_ohsT_e", sub, c, k)])
                    t.wait_ge(V.h, mile["simdiag"])
                    mile[("rt_R_t", sub, c, k)] = mk(TE, t.matmul(
                        sm_b, simd_sb[:], ohsT[:, k, :], start=True, stop=True))

                @on_s
                def _(s, k=k):
                    s.wait_ge(TE.h, mile[("rt_R_t", sub, c, k)])
                    mile[("rt_R_e", sub, c, k)] = mk(SC, s.copy(Rsb[:, k, :], sm_b))

                @on_t
                def _(t, k=k):
                    t.wait_ge(SC.h, mile[("rt_R_e", sub, c, k)])
                    mile[("rt_G_t", sub, c, k)] = mk(TE, t.transpose(
                        sm_c, Rsb[:, k, :], ident32[:]))

                @on_s
                def _(s, k=k):
                    s.wait_ge(TE.h, mile[("rt_G_t", sub, c, k)])
                    mile[("rt_G_e", sub, c, k)] = mk(SC, s.copy(Gk[:, k, :], sm_c))

            @on_v
            def _(v):
                for k in range(4):
                    v.wait_ge(SC.h, mile[("rt_G_e", sub, c, k)])
                    for j in range(4):
                        mk(V, v.tensor_mul(tmp32[:], Gk[:, k, :], ohp[:, j, :]))
                        v.wait_ge(V.h, V.v)
                        mk(V, v.reduce_sum(scand[:, 4 * k + j:4 * k + j + 1],
                                           tmp32[:], axis=AX.X))
                        v.wait_ge(V.h, V.v)
                mk(V, v.memset(cw[:, c, :], 0.0))
                v.wait_ge(V.h, V.v)
                for j in range(4):
                    mk(V, v.tensor_single_scalar(ohb[:], iota32[:], topif[:, j:j + 1],
                                                 op=ALU.is_equal))
                    v.wait_ge(V.h, V.v)
                    mk(V, v.scalar_tensor_tensor(cw[:, c, :], ohb[:], w8[:, j:j + 1],
                                                 cw[:, c, :], op0=ALU.mult, op1=ALU.add))
                    v.wait_ge(V.h, V.v)
                for k in range(4):
                    sl = scand[:, 4 * k:4 * k + 4]
                    mk(V, v.reduce_max(st[:, 11:12], sl, axis=AX.X))
                    v.wait_ge(V.h, V.v)
                    mk(V, v.tensor_single_scalar(tmpk[:, 0:4], sl, st[:, 11:12],
                                                 op=ALU.is_ge))
                    v.wait_ge(V.h, V.v)
                    mk(V, v.tensor_mul(tmpk[:, 4:8], tmpk[:, 0:4], iota4m[:]))
                    v.wait_ge(V.h, V.v)
                    mk(V, v.tensor_reduce(st[:, 12:13], tmpk[:, 4:8], axis=AX.X,
                                          op=ALU.min))
                    v.wait_ge(V.h, V.v)
                    mk(V, v.tensor_single_scalar(tmpk[:, 0:4], iota4m[:], st[:, 12:13],
                                                 op=ALU.is_equal))
                    v.wait_ge(V.h, V.v)
                    mk(V, v.tensor_mul(tmpk[:, 4:8], tmpk[:, 0:4], topif[:, 0:4]))
                    v.wait_ge(V.h, V.v)
                    mk(V, v.reduce_sum(st[:, 13:14], tmpk[:, 4:8], axis=AX.X))
                    v.wait_ge(V.h, V.v)
                    mk(V, v.tensor_single_scalar(st[:, 14:15], st[:, 11:12], 0.5,
                                                 op=ALU.is_lt))
                    v.wait_ge(V.h, V.v)
                    mk(V, v.tensor_mul(st[:, 15:16], st[:, 14:15],
                                       topif[:, 4 + k:5 + k]))
                    v.wait_ge(V.h, V.v)
                    mk(V, v.tensor_scalar(st[:, 16:17], st[:, 14:15], -1.0, 1.0,
                                          op0=ALU.mult, op1=ALU.add))
                    v.wait_ge(V.h, V.v)
                    mk(V, v.tensor_mul(st[:, 16:17], st[:, 16:17], st[:, 13:14]))
                    v.wait_ge(V.h, V.v)
                    mk(V, v.tensor_add(st[:, 15:16], st[:, 15:16], st[:, 16:17]))
                    v.wait_ge(V.h, V.v)
                    mk(V, v.tensor_single_scalar(ohb[:], iota32[:], st[:, 15:16],
                                                 op=ALU.is_equal))
                    v.wait_ge(V.h, V.v)
                    mk(V, v.scalar_tensor_tensor(cw[:, c, :], ohb[:],
                                                 w8[:, 4 + k:5 + k], cw[:, c, :],
                                                 op0=ALU.mult, op1=ALU.add))
                    v.wait_ge(V.h, V.v)
                mile[("rt_done", sub, c)] = V.v

        ffn_gidx = [0]
        ffn_eidx = [0]

        def emit_ffn(sub):
            for c in range(NTC):
                emit_ln_chunk(c, sub)
                emit_router_chunk(c, sub)
            rt_all = mile[("rt_done", sub, NTC - 1)]
            zt_all = mile[("zT_done", sub, NTC - 1)]
            sq_all = mile[("ln_ssq", sub, NTC - 1)]
            rtsc_all = SC.v

            for e in range(NE):
                einst = ffn_eidx[0]
                ffn_eidx[0] += 1

                for h in range(2):
                    gidx = ffn_gidx[0]
                    ffn_gidx[0] += 1
                    par = gidx % 2

                    @on_y
                    def _(y, e=e, h=h, gidx=gidx, par=par):
                        if sub == 0:
                            y.wait_ge(LG.h, mile[("cast", e)])
                        if ("mm1_half", gidx - 2) in mile:
                            y.wait_ge(TE.h, mile[("mm1_half", gidx - 2)])
                        for dc in range(NDC):
                            y.dma_start_transpose(
                                wgT[:, par, dc, 0:384],
                                gup_bf[e, 384 * h:384 * (h + 1),
                                       dc * 128:(dc + 1) * 128],
                            ).then_inc(LY.h, 16)
                            y.dma_start_transpose(
                                wgT[:, par, dc, 384:768],
                                gup_bf[e, 768 + 384 * h:768 + 384 * (h + 1),
                                       dc * 128:(dc + 1) * 128],
                            ).then_inc(LY.h, 16)
                            LY.v += 32
                        mile[("wg_ld", gidx)] = LY.v

                    @on_t
                    def _(t, e=e, h=h, gidx=gidx, par=par):
                        t.wait_ge(LY.h, mile[("wg_ld", gidx)])
                        if h == 0 and e == 0:
                            t.wait_ge(SC.h, sq_all)
                            t.wait_ge(SC.h, zt_all)
                            t.wait_ge(V.h, rt_all)
                        for fcw in range(3):
                            if ("mm1_silu", gidx, fcw - 1) in mile:
                                t.wait_ge(SC.h, mile[("mm1_silu", gidx, fcw - 1)])
                            elif fcw == 0 and ("mm1_silu", gidx - 1, 2) in mile:
                                t.wait_ge(SC.h, mile[("mm1_silu", gidx - 1, 2)])
                            mm = None
                            for dc in range(NDC):
                                for th in range(2):
                                    mm = t.matmul(
                                        PS1[:, th * 512:(th + 1) * 512],
                                        wgT[:, par, dc, fcw * 128:(fcw + 1) * 128],
                                        zT[:, dc, th * 512:(th + 1) * 512],
                                        start=(dc == 0), stop=(dc == NDC - 1))
                            mile[("mm1_g", gidx, fcw)] = mk(TE, mm)
                            if ("mm1_hm", gidx, fcw - 1) in mile:
                                t.wait_ge(V.h, mile[("mm1_hm", gidx, fcw - 1)])
                            elif fcw == 0 and ("mm1_hm", gidx - 1, 2) in mile:
                                t.wait_ge(V.h, mile[("mm1_hm", gidx - 1, 2)])
                            mm = None
                            for dc in range(NDC):
                                for th in range(2):
                                    mm = t.matmul(
                                        PS1[:, (2 + th) * 512:(3 + th) * 512],
                                        wgT[:, par, dc,
                                            384 + fcw * 128:384 + (fcw + 1) * 128],
                                        zT[:, dc, th * 512:(th + 1) * 512],
                                        start=(dc == 0), stop=(dc == NDC - 1))
                            mile[("mm1_u", gidx, fcw)] = mk(TE, mm)
                        mile[("mm1_half", gidx)] = TE.v

                    @on_s
                    def _(s, gidx=gidx):
                        for fcw in range(3):
                            s.wait_ge(TE.h, mile[("mm1_g", gidx, fcw)])
                            if ("mm1_hm", gidx, fcw - 1) in mile:
                                s.wait_ge(V.h, mile[("mm1_hm", gidx, fcw - 1)])
                            elif fcw == 0 and ("mm1_hm", gidx - 1, 2) in mile:
                                s.wait_ge(V.h, mile[("mm1_hm", gidx - 1, 2)])
                            s.activation(silu_t[:, 0, :], PS1[:, 0:512], AF.Silu)
                            mm = s.activation(silu_t[:, 1, :], PS1[:, 512:1024],
                                              AF.Silu)
                            mile[("mm1_silu", gidx, fcw)] = mk(SC, mm)

                    @on_v
                    def _(v, gidx=gidx, h=h):
                        for fcw in range(3):
                            fc = h * 3 + fcw
                            v.wait_ge(TE.h, mile[("mm1_u", gidx, fcw)])
                            v.wait_ge(SC.h, mile[("mm1_silu", gidx, fcw)])
                            mm = None
                            for th in range(2):
                                mm = v.tensor_mul(
                                    hT[:, fc, th * 512:(th + 1) * 512],
                                    silu_t[:, th, :],
                                    PS1[:, (2 + th) * 512:(3 + th) * 512])
                            mile[("mm1_hm", gidx, fcw)] = mk(V, mm)

                @on_y
                def _(y, e=e, einst=einst):
                    if sub == 0:
                        y.wait_ge(LG.h, mile[("cast", e)])
                    if ("mm2_done", einst - 1) in mile:
                        y.wait_ge(TE.h, mile[("mm2_done", einst - 1)])
                    for fc in range(6):
                        y.dma_start_transpose(
                            wdT[:, fc, :], dwn_bf[e, :, fc * 128:(fc + 1) * 128],
                        ).then_inc(LY.h, 16)
                        LY.v += 16
                    mile[("wd_ld", einst)] = LY.v

                @on_t
                def _(t, e=e, einst=einst):
                    t.wait_ge(LY.h, mile[("wd_ld", einst)])
                    t.wait_ge(V.h, mile[("mm1_hm", 2 * einst + 1, 2)])
                    if e == 0:
                        t.wait_ge(V.h, rt_all)
                        t.wait_ge(SC.h, rtsc_all)
                    for tc in range(NTC):
                        for dgp in range(2):
                            u = tc * 2 + dgp
                            bk = (u % 2) * 2
                            if ("mm2_ev", einst, u - 2) in mile:
                                t.wait_ge(V.h, mile[("mm2_ev", einst, u - 2)])
                            elif u < 2 and ("mm2_ev", einst - 1, 14 + u) in mile:
                                t.wait_ge(V.h, mile[("mm2_ev", einst - 1, 14 + u)])
                            mm = None
                            for fc in range(6):
                                for dgi in range(2):
                                    mm = t.matmul(
                                        PS2[:, (bk + dgi) * 512:(bk + dgi + 1) * 512],
                                        hT[:, fc, tc * 128:(tc + 1) * 128],
                                        wdT[:, fc, (dgp * 2 + dgi) * 512:
                                            (dgp * 2 + dgi + 1) * 512],
                                        start=(fc == 0), stop=(fc == 5))
                            mile[("mm2_u", einst, u)] = mk(TE, mm)
                    mile[("mm2_done", einst)] = TE.v

                @on_v
                def _(v, e=e, einst=einst):
                    for tc in range(NTC):
                        for dgp in range(2):
                            u = tc * 2 + dgp
                            bk = (u % 2) * 2
                            v.wait_ge(TE.h, mile[("mm2_u", einst, u)])
                            mm = None
                            for dgi in range(2):
                                dg = dgp * 2 + dgi
                                mm = v.scalar_tensor_tensor(
                                    x_sb[:, tc, dg * 512:(dg + 1) * 512],
                                    PS2[:, (bk + dgi) * 512:(bk + dgi + 1) * 512],
                                    cw[:, tc, e:e + 1],
                                    x_sb[:, tc, dg * 512:(dg + 1) * 512],
                                    op0=ALU.mult, op1=ALU.add)
                            mile[("mm2_ev", einst, u)] = mk(V, mm)

                if sub == 0 and e + 2 < NE:
                    @on_g
                    def _(g, e=e, einst=einst):
                        g.wait_ge(LY.h, mile[("wd_ld", einst)])
                        g.dma_start(out=gup_bf[e + 2], in_=gup[e + 2]).then_inc(LG.h, 16)
                        g.dma_start(out=dwn_bf[e + 2], in_=dwn[e + 2]).then_inc(LG.h, 16)
                        LG.v += 32
                        mile[("cast", e + 2)] = LG.v
            mile[("ffn_done", sub)] = V.v

        def emit_attn():
            sub = 1
            for c in range(NTC):
                emit_ln_chunk(c, sub)
            zt_all = mile[("zT_done", sub, NTC - 1)]
            sq_all = mile[("ln_ssq", sub, NTC - 1)]

            @on_g
            def _(g):
                g.wait_ge(LG.h, mile["attw"])
                g.dma_start(out=wk_v[:], in_=wkbf).then_inc(LG.h, 16); LG.v += 16
                g.dma_start(out=wvv[:], in_=wvbf).then_inc(LG.h, 16); LG.v += 16
                mile["wkv_cast"] = LG.v

            for tc in range(NTC):
                @on_t
                def _(t, tc=tc):
                    if tc == 0:
                        t.wait_ge(LG.h, mile["wkv_cast"])
                        t.wait_ge(SC.h, zt_all)
                        t.wait_ge(SC.h, sq_all)
                    ko = 1024 if tc % 2 == 0 else 0
                    vo = 1536 if tc % 2 == 0 else 512
                    if ("kv_e", tc - 2) in mile:
                        t.wait_ge(SC.h, mile[("kv_e", tc - 2)])
                    mm = None
                    for dc in range(NDC):
                        t.matmul(PS2[:, ko:ko + 128],
                                 zT[:, dc, tc * 128:(tc + 1) * 128],
                                 wk_v[:, dc, :], start=(dc == 0), stop=(dc == NDC - 1))
                        mm = t.matmul(PS2[:, vo:vo + 128],
                                      zT[:, dc, tc * 128:(tc + 1) * 128],
                                      wvv[:, dc, :], start=(dc == 0),
                                      stop=(dc == NDC - 1))
                    mile[("kv_t", tc)] = mk(TE, mm)

                @on_s
                def _(s, tc=tc):
                    ko = 1024 if tc % 2 == 0 else 0
                    vo = 1536 if tc % 2 == 0 else 512
                    s.wait_ge(TE.h, mile[("kv_t", tc)])
                    s.copy(kv_sb[:, tc, 0:128], PS2[:, ko:ko + 128])
                    mile[("kv_e", tc)] = mk(SC, s.copy(kv_sb[:, tc, 128:256],
                                                       PS2[:, vo:vo + 128]))

            @on_g
            def _(g):
                g.wait_ge(SC.h, mile[("kv_e", NTC - 1)])
                g.dma_start(out=kvl3, in_=kv_sb[:]).then_inc(LG.h, 16); LG.v += 16
                g.wait_ge(LG.h, LG.v)
                cc = g.collective_compute(
                    "AllGather", ALU.bypass,
                    replica_groups=[[0, 1], [2, 3], [4, 5], [6, 7]],
                    ins=[kv_loc.opt()], outs=[kv_all.opt()])
                cc.then_inc(CC.h)
                CC.v += 1
                mile["cc"] = CC.v
                g.dma_start(out=wq_v[:], in_=wqbf[:, :, 0:1024]).then_inc(LG.h, 16)
                LG.v += 16
                mile[("wq_cast", 0)] = LG.v
                g.wait_ge(CC.h, mile["cc"])
                g.wait_ge(TE.h, mile[("kv_t", NTC - 1)])
                g.dma_start(out=v_sb[:], in_=kva_v[:, :, 128:256]).then_inc(LG.h, 16)
                LG.v += 16
                mile["v_ld"] = LG.v

            @on_y
            def _(y):
                y.wait_ge(CC.h, mile["cc"])
                y.wait_ge(TE.h, mile[("kv_t", NTC - 1)])
                y.dma_start_transpose(kT_v, kv_all[:, 0:128]).then_inc(LY.h, 16)
                LY.v += 16
                mile["kT_ld"] = LY.v

            for hg in range(2):
                if hg == 1:
                    @on_g
                    def _(g):
                        g.wait_ge(TE.h, mile[("op_t", 0, 15)])
                        g.dma_start(out=wq_v[:],
                                    in_=wqbf[:, :, 1024:2048]).then_inc(LG.h, 16)
                        LG.v += 16
                        mile[("wq_cast", 1)] = LG.v
                for hc in range(8):
                    @on_t
                    def _(t, hg=hg, hc=hc):
                        if hc == 0:
                            t.wait_ge(LG.h, mile[("wq_cast", hg)])
                            if hg == 0:
                                t.wait_ge(SC.h, mile[("kv_e", NTC - 1)])
                        bk = (hc % 2) * 2
                        if ("q_e", hg, hc - 2) in mile:
                            t.wait_ge(SC.h, mile[("q_e", hg, hc - 2)])
                        elif hg == 1 and hc < 2:
                            t.wait_ge(V.h, mile[("op_ev", 0, 14 + hc)])
                            t.wait_ge(SC.h, mile[("q_e", 0, 6 + hc)])
                        mm = None
                        for dc in range(NDC):
                            for th in range(2):
                                mm = t.matmul(
                                    PS2[:, (bk + th) * 512:(bk + th + 1) * 512],
                                    wq_v[:, dc, hc * 128:(hc + 1) * 128],
                                    zT[:, dc, th * 512:(th + 1) * 512],
                                    start=(dc == 0), stop=(dc == NDC - 1))
                        mile[("q_t", hg, hc)] = mk(TE, mm)

                    @on_s
                    def _(s, hg=hg, hc=hc):
                        bk = (hc % 2) * 2
                        s.wait_ge(TE.h, mile[("q_t", hg, hc)])
                        mm = None
                        for th in range(2):
                            mm = s.activation(qT[:, hc, th * 512:(th + 1) * 512],
                                              PS2[:, (bk + th) * 512:(bk + th + 1) * 512],
                                              AF.Copy, scale=RSQ)
                        mile[("q_e", hg, hc)] = mk(SC, mm)

                @on_g
                def _(g, hg=hg):
                    g.wait_ge(TE.h, mile[("q_t", hg, 7)])
                    g.dma_start(out=wo_v[:], in_=wobf[hg]).then_inc(LG.h, 16)
                    LG.v += 16
                    mile[("wo_cast", hg)] = LG.v

                for hc in range(8):
                    for gq in range(2):
                        nblk = 4 * gq + 4

                        @on_v
                        def _(v, hg=hg, hc=hc, gq=gq, nblk=nblk):
                            if ("av_t", hg, hc, gq - 1) in mile:
                                v.wait_ge(TE.h, mile[("av_t", hg, hc, gq - 1)])
                            elif ("av_t", hg, hc - 1, 1) in mile:
                                v.wait_ge(TE.h, mile[("av_t", hg, hc - 1, 1)])
                            elif hg == 1 and ("av_t", 0, 7, 1) in mile:
                                v.wait_ge(TE.h, mile[("av_t", 0, 7, 1)])
                            mm = None
                            for tql in range(4):
                                ext = 4 * gq + tql + 1
                                for b in range(ext, nblk):
                                    mm = v.memset(AT[:, b, tql * 128:(tql + 1) * 128], 0.0)
                                    mm = v.memset(AT[:, 8 + b, tql * 128:(tql + 1) * 128],
                                                  0.0)
                            if mm is not None:
                                mile[("at_ms", hg, hc, gq)] = mk(V, mm)
                            else:
                                mile[("at_ms", hg, hc, gq)] = V.v

                        for tql in range(4):
                            tqc = 4 * gq + tql
                            ext = tqc + 1

                            @on_t
                            def _(t, hg=hg, hc=hc, gq=gq, tql=tql, tqc=tqc, ext=ext):
                                for key in [("sm_exp", hg, hc, gq, tql - 1),
                                            ("sm_exp", hg, hc, gq - 1, 3),
                                            ("sm_exp", hg, hc - 1, 1, 3),
                                            ("sm_exp", hg - 1, 7, 1, 3)]:
                                    if key in mile:
                                        t.wait_ge(SC.h, mile[key])
                                        break
                                if hc == 0 and gq == 0 and tql == 0:
                                    t.wait_ge(LY.h, mile["kT_ld"])
                                    if hg == 0:
                                        t.wait_ge(SC.h, sq_all)
                                t.wait_ge(SC.h, mile[("q_e", hg, hc)])
                                mm = None
                                for half in range(2):
                                    pb = half * 1024
                                    n512 = (ext - 1) // 4
                                    for i5 in range(n512):
                                        mm = t.matmul(
                                            PS1[:, pb + i5 * 512:pb + (i5 + 1) * 512],
                                            qT[:, hc, tqc * 128:(tqc + 1) * 128],
                                            kT_v[:, pb + i5 * 512:pb + (i5 + 1) * 512],
                                            start=True, stop=True)
                                    for b in range(n512 * 4, ext):
                                        mm = t.matmul(
                                            PS1[:, pb + b * 128:pb + (b + 1) * 128],
                                            qT[:, hc, tqc * 128:(tqc + 1) * 128],
                                            kT_v[:, pb + b * 128:pb + (b + 1) * 128],
                                            start=True, stop=True)
                                mile[("sc_t", hg, hc, gq, tql)] = mk(TE, mm)

                            @on_v
                            def _(v, hg=hg, hc=hc, gq=gq, tql=tql, ext=ext):
                                v.wait_ge(TE.h, mile[("sc_t", hg, hc, gq, tql)])
                                d = (ext - 1) * 128
                                mk(V, v.tensor_add(PS1[:, d:d + 128],
                                                   PS1[:, d:d + 128], mke[:]))
                                mk(V, v.tensor_add(PS1[:, 1024 + d:1024 + d + 128],
                                                   PS1[:, 1024 + d:1024 + d + 128],
                                                   mko[:]))
                                v.wait_ge(V.h, V.v)
                                mk(V, v.reduce_max(st[:, 20:21], PS1[:, 0:ext * 128],
                                                   axis=AX.X))
                                mk(V, v.reduce_max(st[:, 21:22],
                                                   PS1[:, 1024:1024 + ext * 128],
                                                   axis=AX.X))
                                v.wait_ge(V.h, V.v)
                                mk(V, v.tensor_max(st[:, 22:23], st[:, 20:21],
                                                   st[:, 21:22]))
                                v.wait_ge(V.h, V.v)
                                mk(V, v.tensor_scalar_mul(st[:, 23:24], st[:, 22:23],
                                                          -1.0))
                                mile[("sm_nm", hg, hc, gq, tql)] = V.v

                            @on_s
                            def _(s, hg=hg, hc=hc, gq=gq, tql=tql, ext=ext):
                                s.wait_ge(V.h, mile[("sm_nm", hg, hc, gq, tql)])
                                s.activation(A1[:, 0:ext * 128], PS1[:, 0:ext * 128],
                                             AF.Exp, bias=st[:, 23:24],
                                             accum_out=st[:, 24:25])
                                mm = s.activation(A1[:, 1024:1024 + ext * 128],
                                                  PS1[:, 1024:1024 + ext * 128],
                                                  AF.Exp, bias=st[:, 23:24],
                                                  accum_out=st[:, 25:26])
                                mile[("sm_exp", hg, hc, gq, tql)] = mk(SC, mm)

                            @on_v
                            def _(v, hg=hg, hc=hc, gq=gq, tql=tql, ext=ext):
                                v.wait_ge(SC.h, mile[("sm_exp", hg, hc, gq, tql)])
                                for key in [("at_ld", hg, hc, gq, tql - 1),
                                            ("at_ld", hg, hc, gq - 1, 3),
                                            ("at_ld", hg, hc - 1, 1, 3),
                                            ("at_ld", hg - 1, 7, 1, 3)]:
                                    if key in mile:
                                        v.wait_ge(LY.h, mile[key])
                                        break
                                mk(V, v.tensor_add(st[:, 26:27], st[:, 24:25],
                                                   st[:, 25:26]))
                                v.wait_ge(V.h, V.v)
                                mk(V, v.reciprocal(st[:, 27:28], st[:, 26:27]))
                                v.wait_ge(V.h, V.v)
                                mk(V, v.tensor_scalar_mul(A2[:, 0:ext * 128],
                                                          A1[:, 0:ext * 128],
                                                          st[:, 27:28]))
                                mm = v.tensor_scalar_mul(A2[:, 1024:1024 + ext * 128],
                                                         A1[:, 1024:1024 + ext * 128],
                                                         st[:, 27:28])
                                mile[("sm_n", hg, hc, gq, tql)] = mk(V, mm)

                            @on_y
                            def _(y, hg=hg, hc=hc, gq=gq, tql=tql, ext=ext):
                                y.wait_ge(V.h, mile[("sm_n", hg, hc, gq, tql)])
                                y.wait_ge(V.h, mile[("at_ms", hg, hc, gq)])
                                for b in range(ext):
                                    y.dma_start_transpose(
                                        AT[:, b, tql * 128:(tql + 1) * 128],
                                        A2[:, b * 128:(b + 1) * 128]).then_inc(LY.h, 16)
                                    y.dma_start_transpose(
                                        AT[:, 8 + b, tql * 128:(tql + 1) * 128],
                                        A2[:, 1024 + b * 128:1024 + (b + 1) * 128],
                                    ).then_inc(LY.h, 16)
                                    LY.v += 32
                                mile[("at_ld", hg, hc, gq, tql)] = LY.v

                        @on_t
                        def _(t, hg=hg, hc=hc, gq=gq, nblk=nblk):
                            t.wait_ge(LY.h, mile[("at_ld", hg, hc, gq, 3)])
                            if hc == 0 and gq == 0:
                                t.wait_ge(LG.h, mile["v_ld"])
                                t.wait_ge(SC.h, mile[("q_e", hg, 7)])
                            u = (hc * 2 + gq) % 2
                            ob = PS2[:, u * 512:(u + 1) * 512]
                            for key in [("ot_e", hg, hc, gq - 2),
                                        ("ot_e", hg, hc - 1, gq),
                                        ("ot_e", hg - 1, 7, gq)]:
                                if key in mile:
                                    t.wait_ge(SC.h, mile[key])
                                    break
                            slots = list(range(nblk)) + [8 + b for b in range(nblk)]
                            mm = None
                            for i, sl in enumerate(slots):
                                mm = t.matmul(ob, v_sb[:, sl, :], AT[:, sl, :],
                                              start=(i == 0), stop=(i == len(slots) - 1))
                            mile[("av_t", hg, hc, gq)] = mk(TE, mm)

                        @on_s
                        def _(s, hg=hg, hc=hc, gq=gq):
                            u = (hc * 2 + gq) % 2
                            ob = PS2[:, u * 512:(u + 1) * 512]
                            s.wait_ge(TE.h, mile[("av_t", hg, hc, gq)])
                            mm = s.copy(qT[:, hc, gq * 512:(gq + 1) * 512], ob)
                            mile[("ot_e", hg, hc, gq)] = mk(SC, mm)

                @on_t
                def _(t, hg=hg):
                    t.wait_ge(LG.h, mile[("wo_cast", hg)])
                    t.wait_ge(SC.h, mile[("ot_e", hg, 7, 1)])
                    for tc in range(NTC):
                        for dgp in range(2):
                            u = tc * 2 + dgp
                            bk = (u % 2) * 2
                            if ("op_ev", hg, u - 2) in mile:
                                t.wait_ge(V.h, mile[("op_ev", hg, u - 2)])
                            elif hg == 1 and u < 2:
                                t.wait_ge(V.h, mile[("op_ev", 0, 14 + u)])
                            mm = None
                            for hc in range(8):
                                for dgi in range(2):
                                    mm = t.matmul(
                                        PS2[:, (bk + dgi) * 512:(bk + dgi + 1) * 512],
                                        qT[:, hc, tc * 128:(tc + 1) * 128],
                                        wo_v[:, hc, (dgp * 2 + dgi) * 512:
                                             (dgp * 2 + dgi + 1) * 512],
                                        start=(hc == 0), stop=(hc == 7))
                            mile[("op_t", hg, u)] = mk(TE, mm)

                @on_v
                def _(v, hg=hg):
                    for tc in range(NTC):
                        for dgp in range(2):
                            u = tc * 2 + dgp
                            bk = (u % 2) * 2
                            v.wait_ge(TE.h, mile[("op_t", hg, u)])
                            mm = None
                            for dgi in range(2):
                                dg = dgp * 2 + dgi
                                mm = v.scalar_tensor_tensor(
                                    x_sb[:, tc, dg * 512:(dg + 1) * 512],
                                    PS2[:, (bk + dgi) * 512:(bk + dgi + 1) * 512],
                                    1.0,
                                    x_sb[:, tc, dg * 512:(dg + 1) * 512],
                                    op0=ALU.mult, op1=ALU.add)
                            mile[("op_ev", hg, u)] = mk(V, mm)

            @on_v
            def _(v):
                v.wait_ge(V.h, mile[("op_ev", 1, 15)])
                v.wait_ge(SC.h, mile["borep"])
                mm = None
                for tc in range(NTC):
                    mm = v.scalar_tensor_tensor(x_sb[:, tc, :], borep[:], 1.0,
                                                x_sb[:, tc, :], op0=ALU.mult,
                                                op1=ALU.add)
                mile["attn_done"] = mk(V, mm)

        emit_ffn(0)
        if DBG:
            @on_y
            def _(y):
                y.wait_ge(V.h, mile[("ffn_done", 0)])
                y.dma_start(dbg_outs[0].rearrange("(c p) d -> p c d", p=128),
                            x_sb[:]).then_inc(LY.h, 16)
                LY.v += 16
                mile["dbgst0"] = LY.v
        barrier()
        if DBG:
            @on_t
            def _(t):
                t.wait_ge(LY.h, mile["dbgst0"])
        emit_attn()
        if DBG:
            @on_y
            def _(y):
                y.wait_ge(V.h, mile["attn_done"])
                y.dma_start(dbg_outs[1].rearrange("(c p) d -> p c d", p=128),
                            x_sb[:]).then_inc(LY.h, 16)
                LY.v += 16
                mile["dbgst1"] = LY.v
        barrier()
        if DBG:
            @on_t
            def _(t):
                t.wait_ge(LY.h, mile["dbgst1"])
        emit_ffn(2)
        barrier()

        @on_y
        def _(y):
            y.dma_start(out3, x_sb[:]).then_inc(LY.h, 16)
            LY.v += 16
            y.wait_ge(LY.h, LY.v)

    return nc


_NC_CACHE = {}


def _get_nc():
    key = ("nc", DBG)
    if key not in _NC_CACHE:
        _NC_CACHE[key] = build_nc()
    return _NC_CACHE[key]


def kernel(L0_attn_x, L0_attn_Wq, L0_attn_Wk, L0_attn_Wv, L0_attn_Wo, L0_attn_bo,
           L0_ffn_gate_weight, L0_ffn_gate_up_proj, L0_ffn_down_proj,
           L0_ffn_similarity):
    x = np.ascontiguousarray(np.asarray(L0_attn_x, dtype=np.float32))
    B, T, _ = x.shape

    f = np.arange(128)[None, :]
    p = np.arange(128)[:, None]
    mask_e = np.where(f <= p, 0.0, NEG).astype(np.float32)

    common = {
        "Wq": np.ascontiguousarray(np.asarray(L0_attn_Wq, dtype=np.float32)),
        "Wk": np.ascontiguousarray(np.asarray(L0_attn_Wk, dtype=np.float32)),
        "Wv": np.ascontiguousarray(np.asarray(L0_attn_Wv, dtype=np.float32)),
        "Wo": np.ascontiguousarray(np.asarray(L0_attn_Wo, dtype=np.float32)),
        "bo": np.ascontiguousarray(np.asarray(L0_attn_bo, dtype=np.float32)[None, :]),
        "gw": np.ascontiguousarray(np.asarray(L0_ffn_gate_weight, dtype=np.float32)),
        "gup": np.ascontiguousarray(np.asarray(L0_ffn_gate_up_proj, dtype=np.float32)),
        "dwn": np.ascontiguousarray(np.asarray(L0_ffn_down_proj, dtype=np.float32)),
        "sim": np.ascontiguousarray(np.asarray(L0_ffn_similarity, dtype=np.float32)),
        "mask_ev": mask_e,
    }
    in_maps = []
    for core in range(8):
        b, h = core // 2, core % 2
        mask_o = np.where(f <= p - 1 + h, 0.0, NEG).astype(np.float32)
        m = dict(common)
        m["x_in"] = np.ascontiguousarray(x[b, h::2, :])
        m["mask_od"] = mask_o
        in_maps.append(m)

    nc = _get_nc()
    res = run_bass_kernel_spmd(nc, in_maps, core_ids=list(range(8)))

    full = np.empty((B, T, D), dtype=np.float32)
    for core in range(8):
        b, h = core // 2, core % 2
        full[b, h::2, :] = res.results[core]["out"]
    if DBG:
        kernel.dbg = [np.stack([res.results[c][f"dbg{i}"] for c in range(8)])
                      for i in range(2)]
    return full


# version 07
# speedup vs baseline: 1.2204x; 1.0469x over previous
import numpy as np
import ml_dtypes
from contextlib import ExitStack

import concourse.bass as bass
import concourse.mybir as mybir
from concourse.bass_utils import run_bass_kernel_spmd

F32 = mybir.dt.float32
BF16 = mybir.dt.bfloat16
U32 = mybir.dt.uint32
AF = mybir.ActivationFunctionType
ALU = mybir.AluOpType
AX = mybir.AxisListType

D = 2048
NT = 1024
NTC = 8
NDC = 16
NE = 32
RSQ = float(np.float32(1.0) / np.float32(np.sqrt(np.float32(128.0))))
EPS = 1e-5
NEG = -1.0e9

DBG = False
DBG3 = False


def build_nc():
    nc = bass.Bass("TRN2", target_bir_lowering=False, debug=False)

    x_in = nc.dram_tensor("x_in", [NT, D], F32, kind="ExternalInput").ap()
    Wq = nc.dram_tensor("Wq", [D, D], F32, kind="ExternalInput").ap()
    Wk = nc.dram_tensor("Wk", [D, 128], F32, kind="ExternalInput").ap()
    Wv = nc.dram_tensor("Wv", [D, 128], F32, kind="ExternalInput").ap()
    Wo = nc.dram_tensor("Wo", [D, D], F32, kind="ExternalInput").ap()
    bo = nc.dram_tensor("bo", [1, D], F32, kind="ExternalInput").ap()
    gw = nc.dram_tensor("gw", [NE, D], F32, kind="ExternalInput").ap()
    gup = nc.dram_tensor("gup", [NE, 1536, D], F32, kind="ExternalInput").ap()
    dwn = nc.dram_tensor("dwn", [NE, D, 768], F32, kind="ExternalInput").ap()
    sim = nc.dram_tensor("sim", [NE, NE], F32, kind="ExternalInput").ap()
    mask_ev = nc.dram_tensor("mask_ev", [128, 128], F32, kind="ExternalInput").ap()
    mask_od = nc.dram_tensor("mask_od", [128, 128], F32, kind="ExternalInput").ap()
    out = nc.dram_tensor("out", [NT, D], F32, kind="ExternalOutput").ap()
    dbg_outs = []
    if DBG3:
        d_qT = nc.dram_tensor("d_qT", [128, 8, 1024], F32, kind="ExternalOutput").ap()
        d_kT = nc.dram_tensor("d_kT", [128, 2048], F32, kind="ExternalOutput").ap()
        d_v = nc.dram_tensor("d_v", [128, 16, 128], F32, kind="ExternalOutput").ap()
        d_A2 = nc.dram_tensor("d_A2", [128, 2048], F32, kind="ExternalOutput").ap()
        d_S = nc.dram_tensor("d_S", [128, 1024], F32, kind="ExternalOutput").ap()
        d_st = nc.dram_tensor("d_st", [128, 64], F32, kind="ExternalOutput").ap()
    if DBG:
        for i in range(2):
            dbg_outs.append(nc.dram_tensor(f"dbg{i}", [NT, D], F32,
                                           kind="ExternalOutput").ap())

    gup_bf = nc.dram_tensor("gup_bf", [NE, 1536, D], BF16, kind="Internal").ap()
    dwn_bf = nc.dram_tensor("dwn_bf", [NE, D, 768], BF16, kind="Internal").ap()
    wqbf = nc.dram_tensor("wqbf", [128, NDC, D], BF16, kind="Internal").ap()
    wobf = nc.dram_tensor("wobf", [2, 128, 8, D], BF16, kind="Internal").ap()
    wkbf = nc.dram_tensor("wkbf", [128, NDC, 128], BF16, kind="Internal").ap()
    wvbf = nc.dram_tensor("wvbf", [128, NDC, 128], BF16, kind="Internal").ap()
    kv_loc = nc.dram_tensor("kv_loc", [NT, 256], BF16, kind="Internal").ap()
    kv_all = nc.dram_tensor("kv_all", [2 * NT, 256], BF16, kind="Internal").ap()

    I128b = nc.inline_tensor(np.eye(128, dtype=ml_dtypes.bfloat16), name="I128b").ap()
    I128f = nc.inline_tensor(np.eye(128, dtype=np.float32), name="I128f").ap()
    I32f = nc.inline_tensor(np.eye(32, dtype=np.float32), name="I32f").ap()
    I32b = nc.inline_tensor(np.eye(32, dtype=ml_dtypes.bfloat16), name="I32b").ap()
    ones1b = nc.inline_tensor(np.ones((1, 128), dtype=ml_dtypes.bfloat16), name="ones1b").ap()

    x3 = x_in.rearrange("(c p) d -> p c d", p=128)
    out3 = out.rearrange("(c p) d -> p c d", p=128)
    gup4 = gup.rearrange("e (a r) d -> e a r d", r=384)
    kvl3 = kv_loc.rearrange("(c p) n -> p c n", p=128)
    kva_v = kv_all.rearrange("(c p) n -> p c n", p=128)
    wq_r = Wq.rearrange("(c p) d -> p c d", p=128)
    wk_r = Wk.rearrange("(c p) d -> p c d", p=128)
    wv_r = Wv.rearrange("(c p) d -> p c d", p=128)
    wo_r = Wo.rearrange("(g hc p) d -> g p hc d", g=2, p=128)

    ctx = ExitStack()
    with ctx:
        e_ = ctx.enter_context

        def sbt(name, shape, dt):
            return e_(nc.sbuf_tensor(name, shape, dt))

        x_sb = sbt("x_sb", [128, NTC, D], F32)
        zT = sbt("zT", [128, NDC, NT], BF16)
        zc = sbt("zc", [128, D], BF16)
        AR1 = sbt("AR1", [128, 24576], BF16)
        AR2 = sbt("AR2", [128, 20480], BF16)
        silu_t = sbt("silu_t", [128, 2, 512], F32)
        cw = sbt("cw", [128, NTC, NE], F32)
        gwT = sbt("gwT", [128, NDC, NE], BF16)
        borep = sbt("borep", [128, D], BF16)
        st = sbt("st", [128, 64], F32)
        exps = sbt("exps", [128, NE], F32)
        topv = sbt("topv", [128, 8], F32)
        topiu = sbt("topiu", [128, 8], U32)
        topif = sbt("topif", [128, 8], F32)
        w8 = sbt("w8", [128, 8], F32)
        ohp = sbt("ohp", [128, 4, NE], F32)
        ohb = sbt("ohb", [128, NE], F32)
        scand = sbt("scand", [128, 16], F32)
        tmp32 = sbt("tmp32", [128, NE], F32)
        tmpk = sbt("tmpk", [128, 8], F32)
        iota32 = sbt("iota32", [128, NE], F32)
        iota4m = sbt("iota4m", [128, 4], F32)
        ident_b = sbt("ident_b", [128, 128], BF16)
        ident_f = sbt("ident_f", [128, 128], F32)
        ident32 = sbt("ident32", [32, 32], F32)
        ident32b = sbt("ident32b", [32, 32], BF16)
        ones_b = sbt("ones_b", [1, 128], BF16)
        mke = sbt("mke", [128, 128], F32)
        mko = sbt("mko", [128, 128], F32)
        simd_sb = sbt("simd_sb", [32, 32], F32)
        sim_t = sbt("sim_t", [32, 32], F32)
        d32 = sbt("d32", [32, 32], F32)
        m32 = sbt("m32", [32, 32], F32)
        ohsT = sbt("ohsT", [32, 4, 128], F32)
        Rsb = sbt("Rsb", [32, 4, 128], F32)
        Gk = sbt("Gk", [128, 4, NE], F32)

        gwbf = AR1[0:32, 0:2048]
        bo_bf = AR1[0:1, 4096:6144]

        wgT = AR1[:].rearrange("p (b c f) -> p b c f", b=2, c=NDC)
        wq_v = AR1[:, 0:16384].rearrange("p (c n) -> p c n", c=NDC)
        wo_v = AR1[:, 0:16384].rearrange("p (c n) -> p c n", c=8)
        wk_v = AR1[:, 16384:18432].rearrange("p (c n) -> p c n", c=NDC)
        wvv = AR1[:, 18432:20480].rearrange("p (c n) -> p c n", c=NDC)
        kT_v = AR1[:, 16384:18432]
        v_sb = AR1[:, 18432:20480].rearrange("p (c n) -> p c n", c=NDC)
        kv_sb = AR1[:, 20480:22528].rearrange("p (c n) -> p c n", c=NTC)

        wdT = AR2[:, 0:12288].rearrange("p (c n) -> p c n", c=6)
        hT = AR2[:, 12288:18432].rearrange("p (c n) -> p c n", c=6)
        qT = AR2[:, 0:8192].rearrange("p (c n) -> p c n", c=8)
        AT = AR2[:, 8192:16384].rearrange("p (c n) -> p c n", c=16)
        A1 = AR2[:, 16384:18432]
        A2 = AR2[:, 18432:20480]

        PS1 = e_(nc.psum_tensor("PS1", [128, 2048], F32))
        PS2 = e_(nc.psum_tensor("PS2", [128, 2048], F32))

        def psbf(slot):
            return PS2[:, slot * 512:slot * 512 + 64].bitcast(BF16)

        psbf32 = PS2[:, 0:16].bitcast(BF16)

        class Ctr:
            def __init__(self, name):
                self.h = e_(nc.semaphore(name))
                self.v = 0

        TE = Ctr("TE"); V = Ctr("V"); SC = Ctr("SC"); G = Ctr("G")
        LG = Ctr("LG"); LY = Ctr("LY"); CC = Ctr("CC")

        def mk(ctr, instr, k=1):
            instr.then_inc(ctr.h, k)
            ctr.v += k
            return ctr.v

        block = e_(nc.Block())
        on_t, on_v, on_s, on_g, on_y = (block.tensor, block.vector, block.scalar,
                                        block.gpsimd, block.sync)
        mile = {}

        def barrier():
            tev, vv, scv, gv = TE.v, V.v, SC.v, G.v

            def w(x):
                x.wait_ge(TE.h, tev)
                x.wait_ge(V.h, vv)
                x.wait_ge(SC.h, scv)
                x.wait_ge(G.h, gv)
            on_t(w); on_v(w); on_s(w); on_g(w); on_y(w)

        @on_y
        def _(y):
            y.dma_start(out=x_sb[:], in_=x3).then_inc(LY.h, 16); LY.v += 16
            for src, dst in [(I128b, ident_b[:]), (I128f, ident_f[:]),
                             (I32f, ident32[:]), (I32b, ident32b[:]),
                             (ones1b, ones_b[:]),
                             (mask_ev, mke[:]), (mask_od, mko[:]),
                             (sim, sim_t[:])]:
                y.dma_start(out=dst, in_=src).then_inc(LY.h, 16); LY.v += 16
            mile["p0_loads"] = LY.v

        @on_g
        def _(g):
            mk(G, g.iota(iota32[:], pattern=[[1, NE]], base=0, channel_multiplier=0,
                         allow_small_or_imprecise_dtypes=True))
            mk(G, g.iota(iota4m[:], pattern=[[1, 4]], base=-100, channel_multiplier=0,
                         allow_small_or_imprecise_dtypes=True))
            mk(G, g.iota(d32[:], pattern=[[1, 32]], base=0, channel_multiplier=-1,
                         allow_small_or_imprecise_dtypes=True))
            g.dma_start(out=gwbf, in_=gw).then_inc(LG.h, 16); LG.v += 16
            g.dma_start(out=bo_bf, in_=bo).then_inc(LG.h, 16); LG.v += 16
            mile["gw_cast"] = LG.v
            g.wait_ge(LG.h, LG.v)
            g.dma_start(out=wqbf, in_=wq_r).then_inc(LG.h, 16); LG.v += 16
            g.dma_start(out=wobf[0], in_=wo_r[0]).then_inc(LG.h, 16); LG.v += 16
            g.dma_start(out=wobf[1], in_=wo_r[1]).then_inc(LG.h, 16); LG.v += 16
            g.dma_start(out=wkbf, in_=wk_r).then_inc(LG.h, 16); LG.v += 16
            g.dma_start(out=wvbf, in_=wv_r).then_inc(LG.h, 16); LG.v += 16
            mile["attw"] = LG.v
            g.wait_ge(LG.h, LG.v)
            for e in range(2):
                g.dma_start(out=gup_bf[e], in_=gup[e]).then_inc(LG.h, 16)
                g.dma_start(out=dwn_bf[e], in_=dwn[e]).then_inc(LG.h, 16)
                LG.v += 32
                mile[("cast", e)] = LG.v
                g.wait_ge(LG.h, LG.v)


        @on_v
        def _(v):
            v.wait_ge(G.h, 3)
            v.wait_ge(LY.h, mile["p0_loads"])
            mk(V, v.tensor_single_scalar(m32[:], d32[:], 0.0, op=ALU.is_equal))
            v.wait_ge(V.h, V.v)
            mk(V, v.tensor_scalar(d32[:], m32[:], -1.0, 1.0, op0=ALU.mult, op1=ALU.add))
            v.wait_ge(V.h, V.v)
            mk(V, v.tensor_mul(simd_sb[:], sim_t[:], d32[:]))
            v.wait_ge(V.h, V.v)
            mk(V, v.tensor_add(simd_sb[:], simd_sb[:], m32[:]))
            mile["simdiag"] = V.v

        for dc in range(NDC):
            @on_t
            def _(t, dc=dc):
                if dc == 0:
                    t.wait_ge(LG.h, mile["gw_cast"])
                    t.wait_ge(LY.h, mile["p0_loads"])
                else:
                    t.wait_ge(SC.h, mile[("gwT_e", dc - 1)])
                mile[("gwT_t", dc)] = mk(TE, t.transpose(
                    psbf32, gwbf[:, dc * 128:(dc + 1) * 128], ident32b[:]))

            @on_s
            def _(s, dc=dc):
                s.wait_ge(TE.h, mile[("gwT_t", dc)])
                mile[("gwT_e", dc)] = mk(SC, s.copy(gwT[:, dc, :], psbf32))
        mile["gwT"] = SC.v

        @on_t
        def _(t):
            t.wait_ge(LY.h, mile["p0_loads"])
            t.wait_ge(LG.h, mile["gw_cast"])
            mm = None
            for j in range(4):
                mm = t.matmul(PS1[:, j * 512:(j + 1) * 512], ones_b[:],
                              bo_bf[:, j * 512:(j + 1) * 512], start=True, stop=True)
            mile["borep_mm"] = mk(TE, mm)

        @on_s
        def _(s):
            s.wait_ge(TE.h, mile["borep_mm"])
            mile["borep"] = mk(SC, s.copy(borep[:], PS1[:, 0:2048]))

        barrier()

        def emit_ln_chunk(c, sub):
            @on_v
            def _(v):
                mk(V, v.reduce_sum(st[:, 0:1], x_sb[:, c, :], axis=AX.X))
                mile[("ln_sum", sub, c)] = V.v

            @on_s
            def _(s):
                s.wait_ge(V.h, mile[("ln_sum", sub, c)])
                mk(SC, s.activation(PS1[:, 0:2048], x_sb[:, c, :], AF.Square,
                                    accum_out=st[:, 1:2]))
                mile[("ln_ssq", sub, c)] = SC.v

            @on_v
            def _(v):
                v.wait_ge(SC.h, mile[("ln_ssq", sub, c)])
                mk(V, v.tensor_scalar_mul(st[:, 2:3], st[:, 0:1], -1.0 / D))
                v.wait_ge(V.h, V.v)
                mk(V, v.tensor_mul(st[:, 3:4], st[:, 2:3], st[:, 2:3]))
                v.wait_ge(V.h, V.v)
                mk(V, v.tensor_scalar_mul(st[:, 1:2], st[:, 1:2], 1.0 / D))
                v.wait_ge(V.h, V.v)
                mk(V, v.tensor_sub(st[:, 3:4], st[:, 1:2], st[:, 3:4]))
                v.wait_ge(V.h, V.v)
                mk(V, v.tensor_scalar_add(st[:, 3:4], st[:, 3:4], EPS))
                mile[("ln_var", sub, c)] = V.v

            @on_s
            def _(s):
                s.wait_ge(V.h, mile[("ln_var", sub, c)])
                mk(SC, s.activation(st[:, 4:5], st[:, 3:4], AF.Sqrt))
                mile[("ln_sd", sub, c)] = SC.v

            @on_v
            def _(v):
                v.wait_ge(SC.h, mile[("ln_sd", sub, c)])
                mk(V, v.reciprocal(st[:, 5:6], st[:, 4:5]))
                v.wait_ge(V.h, V.v)
                mk(V, v.tensor_mul(st[:, 6:7], st[:, 2:3], st[:, 5:6]))
                v.wait_ge(V.h, V.v)
                if c > 0:
                    v.wait_ge(TE.h, mile[("zT_t", sub, c - 1, NDC - 1)])
                mk(V, v.tensor_scalar(zc[:], x_sb[:, c, :], st[:, 5:6], st[:, 6:7],
                                      op0=ALU.mult, op1=ALU.add))
                mile[("ln_z", sub, c)] = V.v

            for dc in range(NDC):
                @on_t
                def _(t, dc=dc):
                    if dc == 0:
                        t.wait_ge(V.h, mile[("ln_z", sub, c)])
                    slot = dc % 2
                    if ("zT_e", sub, c, dc - 2) in mile:
                        t.wait_ge(SC.h, mile[("zT_e", sub, c, dc - 2)])
                    elif c > 0 and dc < 2:
                        t.wait_ge(SC.h, mile[("zT_e", sub, c - 1, NDC - 2 + dc)])
                    mile[("zT_t", sub, c, dc)] = mk(TE, t.transpose(
                        psbf(slot), zc[:, dc * 128:(dc + 1) * 128], ident_b[:]))

                @on_s
                def _(s, dc=dc):
                    slot = dc % 2
                    s.wait_ge(TE.h, mile[("zT_t", sub, c, dc)])
                    mile[("zT_e", sub, c, dc)] = mk(SC, s.copy(
                        zT[:, dc, c * 128:(c + 1) * 128], psbf(slot)))
            mile[("zT_done", sub, c)] = SC.v

        def emit_router_chunk(c, sub):
            lgts = PS2[:, 1024:1056]
            sm_a = PS2[0:32, 1536:1664]
            sm_b = PS2[0:32, 1664:1792]
            sm_c = PS2[:, 1792:1824]

            @on_t
            def _(t):
                t.wait_ge(SC.h, mile[("zT_done", sub, c)])
                if ("rt_lg_read", sub, c - 1) in mile:
                    t.wait_ge(SC.h, mile[("rt_lg_read", sub, c - 1)])
                mm = None
                for dc in range(NDC):
                    mm = t.matmul(lgts, zT[:, dc, c * 128:(c + 1) * 128],
                                  gwT[:, dc, :], start=(dc == 0), stop=(dc == NDC - 1))
                mile[("rt_lg", sub, c)] = mk(TE, mm)

            @on_v
            def _(v):
                v.wait_ge(TE.h, mile[("rt_lg", sub, c)])
                mk(V, v.tensor_reduce(st[:, 8:9], lgts, axis=AX.X, op=ALU.max,
                                      negate=True))
                mile[("rt_nm", sub, c)] = V.v

            @on_s
            def _(s):
                s.wait_ge(V.h, mile[("rt_nm", sub, c)])
                mk(SC, s.activation(exps[:], lgts, AF.Exp, bias=st[:, 8:9]))
                mile[("rt_lg_read", sub, c)] = SC.v
                mile[("rt_exp", sub, c)] = SC.v

            @on_v
            def _(v):
                v.wait_ge(SC.h, mile[("rt_exp", sub, c)])
                mk(V, v.max(out=topv[:], in_=exps[:]))
                v.wait_ge(V.h, V.v)
                mk(V, v.max_index(out=topiu[:], in_max=topv[:], in_values=exps[:]))
                v.wait_ge(V.h, V.v)
                mk(V, v.tensor_copy(topif[:], topiu[:]))
                mk(V, v.reduce_sum(st[:, 9:10], topv[:], axis=AX.X))
                v.wait_ge(V.h, V.v)
                mk(V, v.reciprocal(st[:, 10:11], st[:, 9:10]))
                v.wait_ge(V.h, V.v)
                mk(V, v.tensor_scalar(w8[:], topv[:], st[:, 10:11], 0.5,
                                      op0=ALU.mult, op1=ALU.mult))
                for j in range(4):
                    mk(V, v.tensor_single_scalar(ohp[:, j, :], iota32[:],
                                                 topif[:, j:j + 1], op=ALU.is_equal))
                mile[("rt_w8", sub, c)] = V.v

            for k in range(4):
                @on_v
                def _(v, k=k):
                    v.wait_ge(V.h, mile[("rt_w8", sub, c)])
                    if ("rt_ohsT_t", sub, c, k - 1) in mile:
                        v.wait_ge(TE.h, mile[("rt_ohsT_t", sub, c, k - 1)])
                    mk(V, v.tensor_single_scalar(ohb[:], iota32[:],
                                                 topif[:, 4 + k:5 + k], op=ALU.is_equal))
                    mile[("rt_ohs", sub, c, k)] = V.v

                @on_t
                def _(t, k=k):
                    t.wait_ge(V.h, mile[("rt_ohs", sub, c, k)])
                    if ("rt_G_e", sub, c, k - 1) in mile:
                        t.wait_ge(SC.h, mile[("rt_G_e", sub, c, k - 1)])
                    elif ("rt_G_e", sub, c - 1, 3) in mile:
                        t.wait_ge(SC.h, mile[("rt_G_e", sub, c - 1, 3)])
                    mile[("rt_ohsT_t", sub, c, k)] = mk(TE, t.transpose(
                        sm_a, ohb[:, 0:32], ident_f[:]))

                @on_s
                def _(s, k=k):
                    s.wait_ge(TE.h, mile[("rt_ohsT_t", sub, c, k)])
                    mile[("rt_ohsT_e", sub, c, k)] = mk(SC, s.copy(ohsT[:, k, :], sm_a))

                @on_t
                def _(t, k=k):
                    t.wait_ge(SC.h, mile[("rt_ohsT_e", sub, c, k)])
                    t.wait_ge(V.h, mile["simdiag"])
                    mile[("rt_R_t", sub, c, k)] = mk(TE, t.matmul(
                        sm_b, simd_sb[:], ohsT[:, k, :], start=True, stop=True))

                @on_s
                def _(s, k=k):
                    s.wait_ge(TE.h, mile[("rt_R_t", sub, c, k)])
                    mile[("rt_R_e", sub, c, k)] = mk(SC, s.copy(Rsb[:, k, :], sm_b))

                @on_t
                def _(t, k=k):
                    t.wait_ge(SC.h, mile[("rt_R_e", sub, c, k)])
                    mile[("rt_G_t", sub, c, k)] = mk(TE, t.transpose(
                        sm_c, Rsb[:, k, :], ident32[:]))

                @on_s
                def _(s, k=k):
                    s.wait_ge(TE.h, mile[("rt_G_t", sub, c, k)])
                    mile[("rt_G_e", sub, c, k)] = mk(SC, s.copy(Gk[:, k, :], sm_c))

            @on_v
            def _(v):
                for k in range(4):
                    v.wait_ge(SC.h, mile[("rt_G_e", sub, c, k)])
                    for j in range(4):
                        mk(V, v.tensor_mul(tmp32[:], Gk[:, k, :], ohp[:, j, :]))
                        v.wait_ge(V.h, V.v)
                        mk(V, v.reduce_sum(scand[:, 4 * k + j:4 * k + j + 1],
                                           tmp32[:], axis=AX.X))
                        v.wait_ge(V.h, V.v)
                mk(V, v.memset(cw[:, c, :], 0.0))
                v.wait_ge(V.h, V.v)
                for j in range(4):
                    mk(V, v.tensor_single_scalar(ohb[:], iota32[:], topif[:, j:j + 1],
                                                 op=ALU.is_equal))
                    v.wait_ge(V.h, V.v)
                    mk(V, v.scalar_tensor_tensor(cw[:, c, :], ohb[:], w8[:, j:j + 1],
                                                 cw[:, c, :], op0=ALU.mult, op1=ALU.add))
                    v.wait_ge(V.h, V.v)
                for k in range(4):
                    sl = scand[:, 4 * k:4 * k + 4]
                    mk(V, v.reduce_max(st[:, 11:12], sl, axis=AX.X))
                    v.wait_ge(V.h, V.v)
                    mk(V, v.tensor_single_scalar(tmpk[:, 0:4], sl, st[:, 11:12],
                                                 op=ALU.is_ge))
                    v.wait_ge(V.h, V.v)
                    mk(V, v.tensor_mul(tmpk[:, 4:8], tmpk[:, 0:4], iota4m[:]))
                    v.wait_ge(V.h, V.v)
                    mk(V, v.tensor_reduce(st[:, 12:13], tmpk[:, 4:8], axis=AX.X,
                                          op=ALU.min))
                    v.wait_ge(V.h, V.v)
                    mk(V, v.tensor_single_scalar(tmpk[:, 0:4], iota4m[:], st[:, 12:13],
                                                 op=ALU.is_equal))
                    v.wait_ge(V.h, V.v)
                    mk(V, v.tensor_mul(tmpk[:, 4:8], tmpk[:, 0:4], topif[:, 0:4]))
                    v.wait_ge(V.h, V.v)
                    mk(V, v.reduce_sum(st[:, 13:14], tmpk[:, 4:8], axis=AX.X))
                    v.wait_ge(V.h, V.v)
                    mk(V, v.tensor_single_scalar(st[:, 14:15], st[:, 11:12], 0.5,
                                                 op=ALU.is_lt))
                    v.wait_ge(V.h, V.v)
                    mk(V, v.tensor_mul(st[:, 15:16], st[:, 14:15],
                                       topif[:, 4 + k:5 + k]))
                    v.wait_ge(V.h, V.v)
                    mk(V, v.tensor_scalar(st[:, 16:17], st[:, 14:15], -1.0, 1.0,
                                          op0=ALU.mult, op1=ALU.add))
                    v.wait_ge(V.h, V.v)
                    mk(V, v.tensor_mul(st[:, 16:17], st[:, 16:17], st[:, 13:14]))
                    v.wait_ge(V.h, V.v)
                    mk(V, v.tensor_add(st[:, 15:16], st[:, 15:16], st[:, 16:17]))
                    v.wait_ge(V.h, V.v)
                    mk(V, v.tensor_single_scalar(ohb[:], iota32[:], st[:, 15:16],
                                                 op=ALU.is_equal))
                    v.wait_ge(V.h, V.v)
                    mk(V, v.scalar_tensor_tensor(cw[:, c, :], ohb[:],
                                                 w8[:, 4 + k:5 + k], cw[:, c, :],
                                                 op0=ALU.mult, op1=ALU.add))
                    v.wait_ge(V.h, V.v)
                mile[("rt_done", sub, c)] = V.v

        ffn_gidx = [0]
        ffn_eidx = [0]

        def emit_ffn(sub):
            for c in range(NTC):
                emit_ln_chunk(c, sub)
                emit_router_chunk(c, sub)
            rt_all = mile[("rt_done", sub, NTC - 1)]
            zt_all = mile[("zT_done", sub, NTC - 1)]
            sq_all = mile[("ln_ssq", sub, NTC - 1)]
            rtsc_all = SC.v

            for e in range(NE):
                einst = ffn_eidx[0]
                ffn_eidx[0] += 1

                for h in range(2):
                    gidx = ffn_gidx[0]
                    ffn_gidx[0] += 1
                    par = gidx % 2

                    @on_y
                    def _(y, e=e, h=h, gidx=gidx, par=par):
                        if sub == 0:
                            y.wait_ge(LG.h, mile[("cast", e)])
                        if ("mm1_half", gidx - 2) in mile:
                            y.wait_ge(TE.h, mile[("mm1_half", gidx - 2)])
                        for dc in range(NDC):
                            y.dma_start_transpose(
                                wgT[:, par, dc, 0:384],
                                gup_bf[e, 384 * h:384 * (h + 1),
                                       dc * 128:(dc + 1) * 128],
                            ).then_inc(LY.h, 16)
                            y.dma_start_transpose(
                                wgT[:, par, dc, 384:768],
                                gup_bf[e, 768 + 384 * h:768 + 384 * (h + 1),
                                       dc * 128:(dc + 1) * 128],
                            ).then_inc(LY.h, 16)
                            LY.v += 32
                        mile[("wg_ld", gidx)] = LY.v
                        y.wait_ge(LY.h, LY.v)

                    @on_t
                    def _(t, e=e, h=h, gidx=gidx, par=par):
                        t.wait_ge(LY.h, mile[("wg_ld", gidx)])
                        if h == 0 and e == 0:
                            t.wait_ge(SC.h, sq_all)
                            t.wait_ge(SC.h, zt_all)
                            t.wait_ge(V.h, rt_all)
                        for fcw in range(3):
                            if ("mm1_silu", gidx, fcw - 1) in mile:
                                t.wait_ge(SC.h, mile[("mm1_silu", gidx, fcw - 1)])
                            elif fcw == 0 and ("mm1_silu", gidx - 1, 2) in mile:
                                t.wait_ge(SC.h, mile[("mm1_silu", gidx - 1, 2)])
                            mm = None
                            for dc in range(NDC):
                                for th in range(2):
                                    mm = t.matmul(
                                        PS1[:, th * 512:(th + 1) * 512],
                                        wgT[:, par, dc, fcw * 128:(fcw + 1) * 128],
                                        zT[:, dc, th * 512:(th + 1) * 512],
                                        start=(dc == 0), stop=(dc == NDC - 1))
                            mile[("mm1_g", gidx, fcw)] = mk(TE, mm)
                            if ("mm1_hm", gidx, fcw - 1) in mile:
                                t.wait_ge(V.h, mile[("mm1_hm", gidx, fcw - 1)])
                            elif fcw == 0 and ("mm1_hm", gidx - 1, 2) in mile:
                                t.wait_ge(V.h, mile[("mm1_hm", gidx - 1, 2)])
                            mm = None
                            for dc in range(NDC):
                                for th in range(2):
                                    mm = t.matmul(
                                        PS1[:, (2 + th) * 512:(3 + th) * 512],
                                        wgT[:, par, dc,
                                            384 + fcw * 128:384 + (fcw + 1) * 128],
                                        zT[:, dc, th * 512:(th + 1) * 512],
                                        start=(dc == 0), stop=(dc == NDC - 1))
                            mile[("mm1_u", gidx, fcw)] = mk(TE, mm)
                        mile[("mm1_half", gidx)] = TE.v

                    @on_s
                    def _(s, gidx=gidx):
                        for fcw in range(3):
                            s.wait_ge(TE.h, mile[("mm1_g", gidx, fcw)])
                            if ("mm1_hm", gidx, fcw - 1) in mile:
                                s.wait_ge(V.h, mile[("mm1_hm", gidx, fcw - 1)])
                            elif fcw == 0 and ("mm1_hm", gidx - 1, 2) in mile:
                                s.wait_ge(V.h, mile[("mm1_hm", gidx - 1, 2)])
                            s.activation(silu_t[:, 0, :], PS1[:, 0:512], AF.Silu)
                            mm = s.activation(silu_t[:, 1, :], PS1[:, 512:1024],
                                              AF.Silu)
                            mile[("mm1_silu", gidx, fcw)] = mk(SC, mm)

                    @on_v
                    def _(v, gidx=gidx, h=h):
                        for fcw in range(3):
                            fc = h * 3 + fcw
                            v.wait_ge(TE.h, mile[("mm1_u", gidx, fcw)])
                            v.wait_ge(SC.h, mile[("mm1_silu", gidx, fcw)])
                            mm = None
                            for th in range(2):
                                mm = v.tensor_mul(
                                    hT[:, fc, th * 512:(th + 1) * 512],
                                    silu_t[:, th, :],
                                    PS1[:, (2 + th) * 512:(3 + th) * 512])
                            mile[("mm1_hm", gidx, fcw)] = mk(V, mm)

                @on_y
                def _(y, e=e, einst=einst):
                    if sub == 0:
                        y.wait_ge(LG.h, mile[("cast", e)])
                    if ("mm2_done", einst - 1) in mile:
                        y.wait_ge(TE.h, mile[("mm2_done", einst - 1)])
                    for fc in range(6):
                        y.dma_start_transpose(
                            wdT[:, fc, :], dwn_bf[e, :, fc * 128:(fc + 1) * 128],
                        ).then_inc(LY.h, 16)
                        LY.v += 16
                    mile[("wd_ld", einst)] = LY.v
                    y.wait_ge(LY.h, LY.v)

                @on_t
                def _(t, e=e, einst=einst):
                    t.wait_ge(LY.h, mile[("wd_ld", einst)])
                    t.wait_ge(V.h, mile[("mm1_hm", 2 * einst + 1, 2)])
                    if e == 0:
                        t.wait_ge(V.h, rt_all)
                        t.wait_ge(SC.h, rtsc_all)
                    for tc in range(NTC):
                        for dgp in range(2):
                            u = tc * 2 + dgp
                            bk = (u % 2) * 2
                            if ("mm2_ev", einst, u - 2) in mile:
                                t.wait_ge(V.h, mile[("mm2_ev", einst, u - 2)])
                            elif u < 2 and ("mm2_ev", einst - 1, 14 + u) in mile:
                                t.wait_ge(V.h, mile[("mm2_ev", einst - 1, 14 + u)])
                            mm = None
                            for fc in range(6):
                                for dgi in range(2):
                                    mm = t.matmul(
                                        PS2[:, (bk + dgi) * 512:(bk + dgi + 1) * 512],
                                        hT[:, fc, tc * 128:(tc + 1) * 128],
                                        wdT[:, fc, (dgp * 2 + dgi) * 512:
                                            (dgp * 2 + dgi + 1) * 512],
                                        start=(fc == 0), stop=(fc == 5))
                            mile[("mm2_u", einst, u)] = mk(TE, mm)
                    mile[("mm2_done", einst)] = TE.v

                @on_v
                def _(v, e=e, einst=einst):
                    for tc in range(NTC):
                        for dgp in range(2):
                            u = tc * 2 + dgp
                            bk = (u % 2) * 2
                            v.wait_ge(TE.h, mile[("mm2_u", einst, u)])
                            mm = None
                            for dgi in range(2):
                                dg = dgp * 2 + dgi
                                mm = v.scalar_tensor_tensor(
                                    x_sb[:, tc, dg * 512:(dg + 1) * 512],
                                    PS2[:, (bk + dgi) * 512:(bk + dgi + 1) * 512],
                                    cw[:, tc, e:e + 1],
                                    x_sb[:, tc, dg * 512:(dg + 1) * 512],
                                    op0=ALU.mult, op1=ALU.add)
                            mile[("mm2_ev", einst, u)] = mk(V, mm)

                if sub == 0 and e + 2 < NE:
                    @on_g
                    def _(g, e=e, einst=einst):
                        g.wait_ge(LY.h, mile[("wd_ld", einst)])
                        g.dma_start(out=gup_bf[e + 2], in_=gup[e + 2]).then_inc(LG.h, 16)
                        g.dma_start(out=dwn_bf[e + 2], in_=dwn[e + 2]).then_inc(LG.h, 16)
                        LG.v += 32
                        mile[("cast", e + 2)] = LG.v
            mile[("ffn_done", sub)] = V.v

        def emit_attn():
            sub = 1
            attn_unit = [0]
            unit_slot = []
            unit_exp = {}
            for c in range(NTC):
                emit_ln_chunk(c, sub)
            zt_all = mile[("zT_done", sub, NTC - 1)]
            sq_all = mile[("ln_ssq", sub, NTC - 1)]

            @on_g
            def _(g):
                g.wait_ge(LG.h, mile["attw"])
                g.dma_start(out=wk_v[:], in_=wkbf).then_inc(LG.h, 16); LG.v += 16
                g.dma_start(out=wvv[:], in_=wvbf).then_inc(LG.h, 16); LG.v += 16
                mile["wkv_cast"] = LG.v

            for tc in range(NTC):
                @on_t
                def _(t, tc=tc):
                    if tc == 0:
                        t.wait_ge(LG.h, mile["wkv_cast"])
                        t.wait_ge(SC.h, zt_all)
                        t.wait_ge(SC.h, sq_all)
                    ko = 1024 if tc % 2 == 0 else 0
                    vo = 1536 if tc % 2 == 0 else 512
                    if ("kv_e", tc - 2) in mile:
                        t.wait_ge(SC.h, mile[("kv_e", tc - 2)])
                    mm = None
                    for dc in range(NDC):
                        t.matmul(PS2[:, ko:ko + 128],
                                 zT[:, dc, tc * 128:(tc + 1) * 128],
                                 wk_v[:, dc, :], start=(dc == 0), stop=(dc == NDC - 1))
                        mm = t.matmul(PS2[:, vo:vo + 128],
                                      zT[:, dc, tc * 128:(tc + 1) * 128],
                                      wvv[:, dc, :], start=(dc == 0),
                                      stop=(dc == NDC - 1))
                    mile[("kv_t", tc)] = mk(TE, mm)

                @on_s
                def _(s, tc=tc):
                    ko = 1024 if tc % 2 == 0 else 0
                    vo = 1536 if tc % 2 == 0 else 512
                    s.wait_ge(TE.h, mile[("kv_t", tc)])
                    s.copy(kv_sb[:, tc, 0:128], PS2[:, ko:ko + 128])
                    mile[("kv_e", tc)] = mk(SC, s.copy(kv_sb[:, tc, 128:256],
                                                       PS2[:, vo:vo + 128]))

            @on_g
            def _(g):
                g.wait_ge(SC.h, mile[("kv_e", NTC - 1)])
                g.dma_start(out=kvl3, in_=kv_sb[:]).then_inc(LG.h, 16); LG.v += 16
                g.wait_ge(LG.h, LG.v)
                cc = g.collective_compute(
                    "AllGather", ALU.bypass,
                    replica_groups=[[0, 1], [2, 3], [4, 5], [6, 7]],
                    ins=[kv_loc.opt()], outs=[kv_all.opt()])
                cc.then_inc(CC.h)
                CC.v += 1
                mile["cc"] = CC.v
                g.dma_start(out=wq_v[:], in_=wqbf[:, :, 0:1024]).then_inc(LG.h, 16)
                LG.v += 16
                mile[("wq_cast", 0)] = LG.v
                g.wait_ge(CC.h, mile["cc"])
                g.wait_ge(TE.h, mile[("kv_t", NTC - 1)])
                g.dma_start(out=v_sb[:], in_=kva_v[:, :, 128:256]).then_inc(LG.h, 16)
                LG.v += 16
                mile["v_ld"] = LG.v

            @on_y
            def _(y):
                y.wait_ge(CC.h, mile["cc"])
                y.wait_ge(TE.h, mile[("kv_t", NTC - 1)])
                y.dma_start_transpose(kT_v, kv_all[:, 0:128]).then_inc(LY.h, 16)
                LY.v += 16
                mile["kT_ld"] = LY.v

            for hg in range(2):
                if hg == 1:
                    @on_g
                    def _(g):
                        g.wait_ge(TE.h, mile[("op_t", 0, 15)])
                        g.dma_start(out=wq_v[:],
                                    in_=wqbf[:, :, 1024:2048]).then_inc(LG.h, 16)
                        LG.v += 16
                        mile[("wq_cast", 1)] = LG.v
                for hc in range(8):
                    @on_t
                    def _(t, hg=hg, hc=hc):
                        if hc == 0:
                            t.wait_ge(LG.h, mile[("wq_cast", hg)])
                            if hg == 0:
                                t.wait_ge(SC.h, mile[("kv_e", NTC - 1)])
                        bk = (hc % 2) * 2
                        if ("q_e", hg, hc - 2) in mile:
                            t.wait_ge(SC.h, mile[("q_e", hg, hc - 2)])
                        elif hg == 1 and hc < 2:
                            t.wait_ge(V.h, mile[("op_ev", 0, 14 + hc)])
                            t.wait_ge(SC.h, mile[("q_e", 0, 6 + hc)])
                        mm = None
                        for dc in range(NDC):
                            for th in range(2):
                                mm = t.matmul(
                                    PS2[:, (bk + th) * 512:(bk + th + 1) * 512],
                                    wq_v[:, dc, hc * 128:(hc + 1) * 128],
                                    zT[:, dc, th * 512:(th + 1) * 512],
                                    start=(dc == 0), stop=(dc == NDC - 1))
                        mile[("q_t", hg, hc)] = mk(TE, mm)

                    @on_s
                    def _(s, hg=hg, hc=hc):
                        bk = (hc % 2) * 2
                        s.wait_ge(TE.h, mile[("q_t", hg, hc)])
                        mm = None
                        for th in range(2):
                            mm = s.activation(qT[:, hc, th * 512:(th + 1) * 512],
                                              PS2[:, (bk + th) * 512:(bk + th + 1) * 512],
                                              AF.Copy, scale=RSQ)
                        mile[("q_e", hg, hc)] = mk(SC, mm)

                @on_g
                def _(g, hg=hg):
                    g.wait_ge(TE.h, mile[("q_t", hg, 7)])
                    g.dma_start(out=wo_v[:], in_=wobf[hg]).then_inc(LG.h, 16)
                    LG.v += 16
                    mile[("wo_cast", hg)] = LG.v

                def s_slot(sl, ext):
                    if sl == 0:
                        return PS1[:, 0:ext * 128]
                    if sl == 1:
                        return PS1[:, 1024:1024 + ext * 128]
                    return PS2[:, 1024:1024 + ext * 128]

                for hc in range(8):
                    for gq in range(2):
                        nblk = 4 * gq + 4

                        @on_v
                        def _(v, hg=hg, hc=hc, gq=gq, nblk=nblk):
                            if ("av_t", hg, hc, gq - 1) in mile:
                                v.wait_ge(TE.h, mile[("av_t", hg, hc, gq - 1)])
                            elif ("av_t", hg, hc - 1, 1) in mile:
                                v.wait_ge(TE.h, mile[("av_t", hg, hc - 1, 1)])
                            elif hg == 1 and ("av_t", 0, 7, 1) in mile:
                                v.wait_ge(TE.h, mile[("av_t", 0, 7, 1)])
                            mm = None
                            for tql in range(4):
                                ext = 4 * gq + tql + 1
                                for b in range(ext, nblk):
                                    mm = v.memset(AT[:, b, tql * 128:(tql + 1) * 128], 0.0)
                                    mm = v.memset(AT[:, 8 + b, tql * 128:(tql + 1) * 128],
                                                  0.0)
                            if mm is not None:
                                mile[("at_ms", hg, hc, gq)] = mk(V, mm)
                            else:
                                mile[("at_ms", hg, hc, gq)] = V.v

                        for tql in range(4):
                            tqc = 4 * gq + tql
                            ext = tqc + 1
                            pc = tqc % 2
                            c_nme = 20 + 10 * pc
                            c_nmo = 21 + 10 * pc
                            c_se = 22 + 10 * pc
                            c_so = 23 + 10 * pc
                            c_nm = 24 + 10 * pc
                            c_de = 25 + 10 * pc
                            c_do = 26 + 10 * pc
                            c_fe = 27 + 10 * pc
                            c_fo = 28 + 10 * pc
                            c_r = 29 + 10 * pc

                            for half in range(2):
                                uidx = attn_unit[0]
                                attn_unit[0] += 1
                                sl = uidx % 3
                                unit_slot.append((uidx, sl))

                                @on_t
                                def _(t, hg=hg, hc=hc, tqc=tqc, half=half, ext=ext,
                                      sl=sl, uidx=uidx):
                                    if uidx - 3 in unit_exp:
                                        t.wait_ge(SC.h, unit_exp[uidx - 3])
                                    if hc == 0 and tqc == 0 and half == 0:
                                        t.wait_ge(LY.h, mile["kT_ld"])
                                        if hg == 0:
                                            t.wait_ge(SC.h, sq_all)
                                    if sl == 2 and ("s2_init", hg) not in mile:
                                        t.wait_ge(SC.h, mile[("q_e", hg, 7)])
                                        if hg == 1:
                                            t.wait_ge(V.h, mile[("op_ev", 0, 15)])
                                        mile[("s2_init", hg)] = 1
                                    t.wait_ge(SC.h, mile[("q_e", hg, hc)])
                                    Su = s_slot(sl, ext)
                                    kbase = half * 1024
                                    mm = None
                                    n512 = (ext - 1) // 4
                                    for i5 in range(n512):
                                        mm = t.matmul(
                                            Su[:, i5 * 512:(i5 + 1) * 512],
                                            qT[:, hc, tqc * 128:(tqc + 1) * 128],
                                            kT_v[:, kbase + i5 * 512:kbase + (i5 + 1) * 512],
                                            start=True, stop=True)
                                    for b in range(n512 * 4, ext):
                                        mm = t.matmul(
                                            Su[:, b * 128:(b + 1) * 128],
                                            qT[:, hc, tqc * 128:(tqc + 1) * 128],
                                            kT_v[:, kbase + b * 128:kbase + (b + 1) * 128],
                                            start=True, stop=True)
                                    mile[("sc_t", hg, hc, tqc, half)] = mk(TE, mm)

                                @on_v
                                def _(v, hg=hg, hc=hc, tqc=tqc, half=half, ext=ext,
                                      sl=sl, cn=(c_nme if half == 0 else c_nmo)):
                                    v.wait_ge(TE.h, mile[("sc_t", hg, hc, tqc, half)])
                                    Su = s_slot(sl, ext)
                                    d = (ext - 1) * 128
                                    mk(V, v.tensor_add(Su[:, d:d + 128], Su[:, d:d + 128],
                                                       mke[:] if half == 0 else mko[:]))
                                    mk(V, v.tensor_reduce(st[:, cn:cn + 1], Su,
                                                          axis=AX.X, op=ALU.max,
                                                          negate=True))
                                    mile[("sm_nm", hg, hc, tqc, half)] = V.v

                                @on_s
                                def _(s, hg=hg, hc=hc, tqc=tqc, half=half, ext=ext,
                                      sl=sl, uidx=uidx,
                                      cn=(c_nme if half == 0 else c_nmo),
                                      cs=(c_se if half == 0 else c_so)):
                                    s.wait_ge(V.h, mile[("sm_nm", hg, hc, tqc, half)])
                                    if ("sm_n", hg, hc, tqc - 1) in mile:
                                        s.wait_ge(V.h, mile[("sm_n", hg, hc, tqc - 1)])
                                    elif tqc == 0 and ("sm_n", hg, hc - 1, 7) in mile:
                                        s.wait_ge(V.h, mile[("sm_n", hg, hc - 1, 7)])
                                    elif tqc == 0 and hc == 0 and hg == 1:
                                        s.wait_ge(V.h, mile[("sm_n", 0, 7, 7)])
                                    for key in [("sm_r", hg, hc, tqc - 2),
                                                ("sm_r", hg, hc - 1, tqc + 6),
                                                ("sm_r", hg - 1, 7, tqc + 6)]:
                                        if key in mile:
                                            s.wait_ge(V.h, mile[key])
                                            break
                                    off = half * 1024
                                    mm = s.activation(A1[:, off:off + ext * 128],
                                                      s_slot(sl, ext), AF.Exp,
                                                      bias=st[:, cn:cn + 1],
                                                      accum_out=st[:, cs:cs + 1])
                                    unit_exp[uidx] = mk(SC, mm)
                                    mile[("sm_exp", hg, hc, tqc, half)] = unit_exp[uidx]

                            @on_v
                            def _(v, hg=hg, hc=hc, tqc=tqc, ext=ext,
                                  c_nme=c_nme, c_nmo=c_nmo, c_nm=c_nm, c_de=c_de,
                                  c_do=c_do, c_fe=c_fe, c_fo=c_fo, c_r=c_r,
                                  c_se=c_se, c_so=c_so):
                                v.wait_ge(SC.h, mile[("sm_exp", hg, hc, tqc, 1)])
                                mk(V, v.tensor_tensor(st[:, c_nm:c_nm + 1],
                                                      st[:, c_nme:c_nme + 1],
                                                      st[:, c_nmo:c_nmo + 1],
                                                      op=ALU.min))
                                v.wait_ge(V.h, V.v)
                                mk(V, v.tensor_sub(st[:, c_de:c_de + 1],
                                                   st[:, c_nm:c_nm + 1],
                                                   st[:, c_nme:c_nme + 1]))
                                mm = v.tensor_sub(st[:, c_do:c_do + 1],
                                                  st[:, c_nm:c_nm + 1],
                                                  st[:, c_nmo:c_nmo + 1])
                                mile[("sm_j1", hg, hc, tqc)] = mk(V, mm)

                            @on_s
                            def _(s, hg=hg, hc=hc, tqc=tqc,
                                  c_nme=c_nme, c_nmo=c_nmo, c_nm=c_nm,
                                  c_de=c_de, c_do=c_do, c_fe=c_fe, c_fo=c_fo):
                                s.wait_ge(V.h, mile[("sm_j1", hg, hc, tqc)])
                                s.activation(st[:, c_fe:c_fe + 1],
                                             st[:, c_de:c_de + 1], AF.Exp)
                                mm = s.activation(st[:, c_fo:c_fo + 1],
                                                  st[:, c_do:c_do + 1], AF.Exp)
                                mile[("sm_f", hg, hc, tqc)] = mk(SC, mm)

                            @on_v
                            def _(v, hg=hg, hc=hc, tqc=tqc, ext=ext,
                                  c_fe=c_fe, c_fo=c_fo, c_r=c_r, c_se=c_se,
                                  c_so=c_so, c_de=c_de, c_do=c_do):
                                v.wait_ge(SC.h, mile[("sm_f", hg, hc, tqc)])
                                mk(V, v.tensor_mul(st[:, c_de:c_de + 1],
                                                   st[:, c_se:c_se + 1],
                                                   st[:, c_fe:c_fe + 1]))
                                mk(V, v.scalar_tensor_tensor(st[:, c_do:c_do + 1],
                                                             st[:, c_so:c_so + 1],
                                                             1.0,
                                                             st[:, c_fo:c_fo + 1],
                                                             op0=ALU.mult,
                                                             op1=ALU.mult))
                                v.wait_ge(V.h, V.v)
                                mk(V, v.tensor_add(st[:, c_r:c_r + 1],
                                                   st[:, c_de:c_de + 1],
                                                   st[:, c_do:c_do + 1]))
                                v.wait_ge(V.h, V.v)
                                mk(V, v.reciprocal(st[:, c_de:c_de + 1],
                                                   st[:, c_r:c_r + 1]))
                                v.wait_ge(V.h, V.v)
                                mk(V, v.tensor_mul(st[:, c_fe:c_fe + 1],
                                                   st[:, c_fe:c_fe + 1],
                                                   st[:, c_de:c_de + 1]))
                                mk(V, v.tensor_mul(st[:, c_fo:c_fo + 1],
                                                   st[:, c_fo:c_fo + 1],
                                                   st[:, c_de:c_de + 1]))
                                v.wait_ge(V.h, V.v)
                                for key in [("at_ld", hg, hc, tqc - 1),
                                            ("at_ld", hg, hc - 1, 7),
                                            ("at_ld", hg - 1, 7, 7)]:
                                    if key in mile:
                                        v.wait_ge(LY.h, mile[key])
                                        break
                                mk(V, v.tensor_scalar_mul(A2[:, 0:ext * 128],
                                                          A1[:, 0:ext * 128],
                                                          st[:, c_fe:c_fe + 1]))
                                mm = v.tensor_scalar_mul(
                                    A2[:, 1024:1024 + ext * 128],
                                    A1[:, 1024:1024 + ext * 128],
                                    st[:, c_fo:c_fo + 1])
                                mile[("sm_r", hg, hc, tqc)] = mk(V, mm)
                                mile[("sm_n", hg, hc, tqc)] = V.v

                            if DBG3 and hg == 0 and hc == 0 and tqc == 7:
                                @on_g
                                def _(g):
                                    g.wait_ge(V.h, mile[("sm_n", 0, 0, 7)])
                                    g.dma_start(out=d_A2, in_=A2[:]).then_inc(LG.h, 16)
                                    g.dma_start(out=d_qT, in_=qT[:]).then_inc(LG.h, 16)
                                    g.dma_start(out=d_kT, in_=kT_v).then_inc(LG.h, 16)
                                    g.dma_start(out=d_v, in_=v_sb[:]).then_inc(LG.h, 16)
                                    g.dma_start(out=d_st, in_=st[:]).then_inc(LG.h, 16)
                                    LG.v += 80
                                    mile["d3"] = LG.v

                                @on_v
                                def _(v):
                                    v.wait_ge(LG.h, mile["d3"])
                                    mk(V, v.engine_nop())

                            @on_y
                            def _(y, hg=hg, hc=hc, gq=gq, tql=tql, tqc=tqc, ext=ext):
                                y.wait_ge(V.h, mile[("sm_n", hg, hc, tqc)])
                                y.wait_ge(V.h, mile[("at_ms", hg, hc, gq)])
                                for b in range(ext):
                                    y.dma_start_transpose(
                                        AT[:, b, tql * 128:(tql + 1) * 128],
                                        A2[:, b * 128:(b + 1) * 128]).then_inc(LY.h, 16)
                                    y.dma_start_transpose(
                                        AT[:, 8 + b, tql * 128:(tql + 1) * 128],
                                        A2[:, 1024 + b * 128:1024 + (b + 1) * 128],
                                    ).then_inc(LY.h, 16)
                                    LY.v += 32
                                mile[("at_ld", hg, hc, tqc)] = LY.v

                        @on_t
                        def _(t, hg=hg, hc=hc, gq=gq, nblk=nblk):
                            t.wait_ge(LY.h, mile[("at_ld", hg, hc, 4 * gq + 3)])
                            if hc == 0 and gq == 0:
                                t.wait_ge(LG.h, mile["v_ld"])
                                t.wait_ge(SC.h, mile[("q_e", hg, 7)])
                            u = (hc * 2 + gq) % 2
                            ob = PS2[:, u * 512:(u + 1) * 512]
                            for key in [("ot_e", hg, hc, gq - 2),
                                        ("ot_e", hg, hc - 1, gq),
                                        ("ot_e", hg - 1, 7, gq)]:
                                if key in mile:
                                    t.wait_ge(SC.h, mile[key])
                                    break
                            slots = list(range(nblk)) + [8 + b for b in range(nblk)]
                            mm = None
                            for i, sl in enumerate(slots):
                                mm = t.matmul(ob, v_sb[:, sl, :], AT[:, sl, :],
                                              start=(i == 0), stop=(i == len(slots) - 1))
                            mile[("av_t", hg, hc, gq)] = mk(TE, mm)

                        @on_s
                        def _(s, hg=hg, hc=hc, gq=gq):
                            u = (hc * 2 + gq) % 2
                            ob = PS2[:, u * 512:(u + 1) * 512]
                            s.wait_ge(TE.h, mile[("av_t", hg, hc, gq)])
                            mm = s.copy(qT[:, hc, gq * 512:(gq + 1) * 512], ob)
                            mile[("ot_e", hg, hc, gq)] = mk(SC, mm)

                @on_t
                def _(t, hg=hg):
                    t.wait_ge(LG.h, mile[("wo_cast", hg)])
                    t.wait_ge(SC.h, mile[("ot_e", hg, 7, 1)])
                    for tc in range(NTC):
                        for dgp in range(2):
                            u = tc * 2 + dgp
                            bk = (u % 2) * 2
                            if ("op_ev", hg, u - 2) in mile:
                                t.wait_ge(V.h, mile[("op_ev", hg, u - 2)])
                            elif hg == 1 and u < 2:
                                t.wait_ge(V.h, mile[("op_ev", 0, 14 + u)])
                            mm = None
                            for hc in range(8):
                                for dgi in range(2):
                                    mm = t.matmul(
                                        PS2[:, (bk + dgi) * 512:(bk + dgi + 1) * 512],
                                        qT[:, hc, tc * 128:(tc + 1) * 128],
                                        wo_v[:, hc, (dgp * 2 + dgi) * 512:
                                             (dgp * 2 + dgi + 1) * 512],
                                        start=(hc == 0), stop=(hc == 7))
                            mile[("op_t", hg, u)] = mk(TE, mm)

                @on_v
                def _(v, hg=hg):
                    for tc in range(NTC):
                        for dgp in range(2):
                            u = tc * 2 + dgp
                            bk = (u % 2) * 2
                            v.wait_ge(TE.h, mile[("op_t", hg, u)])
                            mm = None
                            for dgi in range(2):
                                dg = dgp * 2 + dgi
                                mm = v.scalar_tensor_tensor(
                                    x_sb[:, tc, dg * 512:(dg + 1) * 512],
                                    PS2[:, (bk + dgi) * 512:(bk + dgi + 1) * 512],
                                    1.0,
                                    x_sb[:, tc, dg * 512:(dg + 1) * 512],
                                    op0=ALU.mult, op1=ALU.add)
                            mile[("op_ev", hg, u)] = mk(V, mm)

            @on_v
            def _(v):
                v.wait_ge(V.h, mile[("op_ev", 1, 15)])
                v.wait_ge(SC.h, mile["borep"])
                mm = None
                for tc in range(NTC):
                    mm = v.scalar_tensor_tensor(x_sb[:, tc, :], borep[:], 1.0,
                                                x_sb[:, tc, :], op0=ALU.mult,
                                                op1=ALU.add)
                mile["attn_done"] = mk(V, mm)

        emit_ffn(0)
        if DBG:
            @on_y
            def _(y):
                y.wait_ge(V.h, mile[("ffn_done", 0)])
                y.dma_start(dbg_outs[0].rearrange("(c p) d -> p c d", p=128),
                            x_sb[:]).then_inc(LY.h, 16)
                LY.v += 16
                mile["dbgst0"] = LY.v
        barrier()
        if DBG:
            @on_t
            def _(t):
                t.wait_ge(LY.h, mile["dbgst0"])
        emit_attn()
        if DBG:
            @on_y
            def _(y):
                y.wait_ge(V.h, mile["attn_done"])
                y.dma_start(dbg_outs[1].rearrange("(c p) d -> p c d", p=128),
                            x_sb[:]).then_inc(LY.h, 16)
                LY.v += 16
                mile["dbgst1"] = LY.v
        barrier()
        if DBG:
            @on_t
            def _(t):
                t.wait_ge(LY.h, mile["dbgst1"])
        emit_ffn(2)
        barrier()

        @on_y
        def _(y):
            y.dma_start(out3, x_sb[:]).then_inc(LY.h, 16)
            LY.v += 16
            y.wait_ge(LY.h, LY.v)

    return nc


_NC_CACHE = {}


def _get_nc():
    key = ("nc", DBG)
    if key not in _NC_CACHE:
        _NC_CACHE[key] = build_nc()
    return _NC_CACHE[key]


def kernel(L0_attn_x, L0_attn_Wq, L0_attn_Wk, L0_attn_Wv, L0_attn_Wo, L0_attn_bo,
           L0_ffn_gate_weight, L0_ffn_gate_up_proj, L0_ffn_down_proj,
           L0_ffn_similarity):
    x = np.ascontiguousarray(np.asarray(L0_attn_x, dtype=np.float32))
    B, T, _ = x.shape

    f = np.arange(128)[None, :]
    p = np.arange(128)[:, None]
    mask_e = np.where(f <= p, 0.0, NEG).astype(np.float32)

    common = {
        "Wq": np.ascontiguousarray(np.asarray(L0_attn_Wq, dtype=np.float32)),
        "Wk": np.ascontiguousarray(np.asarray(L0_attn_Wk, dtype=np.float32)),
        "Wv": np.ascontiguousarray(np.asarray(L0_attn_Wv, dtype=np.float32)),
        "Wo": np.ascontiguousarray(np.asarray(L0_attn_Wo, dtype=np.float32)),
        "bo": np.ascontiguousarray(np.asarray(L0_attn_bo, dtype=np.float32)[None, :]),
        "gw": np.ascontiguousarray(np.asarray(L0_ffn_gate_weight, dtype=np.float32)),
        "gup": np.ascontiguousarray(np.asarray(L0_ffn_gate_up_proj, dtype=np.float32)),
        "dwn": np.ascontiguousarray(np.asarray(L0_ffn_down_proj, dtype=np.float32)),
        "sim": np.ascontiguousarray(np.asarray(L0_ffn_similarity, dtype=np.float32)),
        "mask_ev": mask_e,
    }
    in_maps = []
    for core in range(8):
        b, h = core // 2, core % 2
        mask_o = np.where(f <= p - 1 + h, 0.0, NEG).astype(np.float32)
        m = dict(common)
        m["x_in"] = np.ascontiguousarray(x[b, h::2, :])
        m["mask_od"] = mask_o
        in_maps.append(m)

    nc = _get_nc()
    res = run_bass_kernel_spmd(nc, in_maps, core_ids=list(range(8)))

    full = np.empty((B, T, D), dtype=np.float32)
    for core in range(8):
        b, h = core // 2, core % 2
        full[b, h::2, :] = res.results[core]["out"]
    if DBG:
        kernel.dbg = [np.stack([res.results[c][f"dbg{i}"] for c in range(8)])
                      for i in range(2)]
    return full
